# version 05
# baseline (speedup 1.0000x reference)
import numpy as np
import ml_dtypes
from contextlib import ExitStack

import concourse.bass as bass
import concourse.mybir as mybir
from concourse.bass_core_rust import IndirectOffsetOnAxis

F32 = mybir.dt.float32
BF16 = mybir.dt.bfloat16
I32 = mybir.dt.int32
AF = mybir.ActivationFunctionType
ALU = mybir.AluOpType
AX = mybir.AxisListType.X

D = 1024
T = 4096
NCH = D // 128
NB = T // 512
NT = T // 128
HEADS = 16
HD = 64


def np2dt(d):
    d = np.dtype(d)
    if d == np.float32:
        return F32
    if d == np.int32:
        return I32
    if d == ml_dtypes.bfloat16:
        return BF16
    raise ValueError(d)


class Eng:

    def __init__(self, k, name, step, stream):
        self.k = k
        self.name = name
        self.step = step
        self.stream = stream
        self.n = 0
        self.waited = {}

    def wait(self, ev):
        if ev is None:
            return
        if isinstance(ev, list) or (isinstance(ev, tuple) and (not ev or isinstance(ev[0], (list, tuple)))):
            for e in ev:
                self.wait(e)
            return
        name, cnt, ep = ev
        if ep < self.k.epoch_id:
            return
        if cnt <= 0 or self.waited.get(name, 0) >= cnt:
            return
        self.waited[name] = cnt
        sem = self.k.sems[name]
        self.stream.append(lambda raw: raw.wait_ge(sem, cnt))

    def op(self, method, *args, **kwargs):
        if self.step == 16 and self.n > 0:
            self.wait((self.name, self.n, self.k.epoch_id))
        self.n += self.step
        n = self.n
        sem = self.k.sems[self.name]
        step = self.step

        def emit(raw):
            getattr(raw, method)(*args, **kwargs).then_inc(sem, step)
        self.stream.append(emit)
        return (self.name, n, self.k.epoch_id)

    def ev(self):
        return (self.name, self.n, self.k.epoch_id)


class K:

    def __init__(self):
        self.nc = bass.Bass("TRN2", target_bir_lowering=False)
        self.st = ExitStack()
        self.sems = {}
        self.streams = {"pe": [], "act": [], "dve": [], "pool": [], "sp": []}
        for name in ["pe", "act", "dve", "pool", "qs", "qp", "qa"]:
            self.sems[name] = self.st.enter_context(self.nc.semaphore(name))
        self.pe = Eng(self, "pe", 1, self.streams["pe"])
        self.act = Eng(self, "act", 1, self.streams["act"])
        self.dve = Eng(self, "dve", 1, self.streams["dve"])
        self.pool = Eng(self, "pool", 1, self.streams["pool"])
        self.qs = Eng(self, "qs", 16, self.streams["sp"])
        self.qp = Eng(self, "qp", 16, self.streams["pool"])
        self.qa = Eng(self, "qa", 16, self.streams["act"])
        self.consts = {}
        self._names = 0
        self.epoch_id = 0
        self.sems["epoch"] = self.st.enter_context(self.nc.semaphore("epoch"))

    def name(self, p):
        self._names += 1
        return f"{p}{self._names}"

    def sb(self, prefix, shape, dtype=BF16):
        return self.st.enter_context(self.nc.sbuf_tensor(self.name(prefix), shape, dtype))

    def ps(self, prefix, shape, dtype=F32):
        return self.st.enter_context(self.nc.psum_tensor(self.name(prefix), shape, dtype))

    def dram(self, nm, shape, dtype=BF16):
        return self.nc.dram_tensor(nm, shape, dtype, kind="Internal").ap()

    def inp(self, nm, shape, dtype=F32):
        return self.nc.dram_tensor(nm, list(shape), dtype, kind="ExternalInput").ap()

    def out(self, nm, shape, dtype=F32):
        return self.nc.dram_tensor(nm, list(shape), dtype, kind="ExternalOutput").ap()

    def const(self, nm, arr):
        if nm not in self.consts:
            a = np.ascontiguousarray(arr)
            self.consts[nm] = a
            self._cap = getattr(self, "_cap", {})
            self._cap[nm] = self.nc.dram_tensor(nm, list(a.shape), np2dt(a.dtype),
                                                kind="ExternalInput").ap()
        return self._cap[nm]

    def finalize(self):
        block = self.st.enter_context(self.nc.Block())

        def run(ops):
            def f(raw):
                for op in ops:
                    op(raw)
            return f
        block.tensor(run(self.streams["pe"]))
        block.scalar(run(self.streams["act"]))
        block.vector(run(self.streams["dve"]))
        block.gpsimd(run(self.streams["pool"]))
        block.sync(run(self.streams["sp"]))
        self.st.close()
        return self.nc


class Scope:
    def __init__(self, k):
        self.k = k
        self.st = ExitStack()

    def __enter__(self):
        self.k._scopes.append(self.st)
        return self

    def __exit__(self, *a):
        self.k._scopes.pop()
        self.k.barrier()
        self.st.close()
        return False


def _k_scope(self):
    return Scope(self)


def _k_epoch(self):
    self.barrier()
    engs = [self.pe, self.act, self.dve, self.pool, self.qs, self.qp, self.qa]
    epoch_sem = self.sems["epoch"]
    sems = [self.sems[nm] for nm in ["pe", "act", "dve", "pool", "qs", "qp", "qa"]]
    self.epoch_id += 1
    target = self.epoch_id

    def clear_all(raw):
        for s in sems:
            raw.sem_clear(s)
        raw.nop().then_inc(epoch_sem, 1)
    self.streams["pool"].append(clear_all)
    for st_name in ["pe", "act", "dve", "sp"]:
        self.streams[st_name].append(
            lambda raw, t=target: raw.wait_ge(epoch_sem, t))
    self.streams["pool"].append(
        lambda raw, t=target: raw.wait_ge(epoch_sem, t))
    for e in engs:
        e.n = 0
        e.waited = {}


def _k_barrier(self):
    engs = [self.pe, self.act, self.dve, self.pool, self.qs, self.qp, self.qa]
    cur = [(e.name, e.n, self.epoch_id) for e in engs]
    for e in [self.pe, self.act, self.dve, self.pool, self.qs]:
        for ev in cur:
            if ev[0] != e.name:
                e.wait(ev)


def _k_sb2(self, prefix, shape, dtype=BF16):
    st = self._scopes[-1] if self._scopes else self.st
    return st.enter_context(self.nc.sbuf_tensor(self.name(prefix), shape, dtype))


def _k_ps2(self, prefix, shape, dtype=F32):
    st = self._scopes[-1] if self._scopes else self.st
    return st.enter_context(self.nc.psum_tensor(self.name(prefix), shape, dtype))


K.scope = _k_scope
K.barrier = _k_barrier
K.epoch = _k_epoch
K.sb = _k_sb2
K.ps = _k_ps2
_old_init = K.__init__


def _k_init(self):
    _old_init(self)
    self._scopes = []


K.__init__ = _k_init


"""Block emitters for the megakernel. See mk.py for conventions.

Key structures:
- Shared: owns all reusable scratch (row pools packed at partitions 0/32/64/96,
  8 half-psum-banks, band tiles) so emitters don't leak SBUF/PSUM.
- emit_attention: flash attention; input provided per band by a callback
  (SBUF stream slices, or DRAM feature-major activation staged per band).
- Softmax stabilization: exact shift-invariance with per-query upper bound
  m_q = (|q_hat|^2 + max_s |k|^2)/2 (AM-GM), carried in augmented row 64 of Q
  with K row 64 = -1.
"""
import numpy as np
import ml_dtypes


BIG = 1.0e9
HD1 = HD + 1


class Bf:
    __slots__ = ("wev", "revs")

    def __init__(self):
        self.wev = None
        self.revs = []


def do(eng, method, dsts, srcs, *args, relaxed=False, **kw):
    for b in dsts:
        if not relaxed:
            eng.wait(b.wev)
        eng.wait(b.revs)
    for b in srcs:
        eng.wait(b.wev)
    ev = eng.op(method, *args, **kw)
    for b in dsts:
        b.wev = ev
        b.revs = []
    for b in srcs:
        b.revs.append(ev)
    return ev


class Row:

    def __init__(self, pool, p, b):
        self.pool = pool
        self.p = p
        self.b = b

    def __getitem__(self, sl):
        if isinstance(sl, tuple):
            sl = sl[1]
        return self.pool[self.p:self.p + 1, sl]

    @property
    def full(self):
        return self.pool[self.p:self.p + 1, :]


class Shared:
    def __init__(self, k):
        self.k = k
        self.ones_col = k.sb("onescol", [128, 1], BF16)
        self.ones_pool = k.sb("onespool", [128, 512], BF16)
        self.ones_col_b = Bf()
        self.ones_row_b = Bf()
        do(k.pool, "memset", [self.ones_col_b], [], self.ones_col[:], 1.0)
        do(k.pool, "memset", [self.ones_row_b], [], self.ones_pool[:], 1.0)
        ident = np.eye(128, dtype=ml_dtypes.bfloat16)
        self.identsb = k.sb("identsb", [128, 128], BF16)
        self.ident_b = Bf()
        do(k.qp, "dma_start", [self.ident_b], [], out=self.identsb[:],
           in_=k.const("c_ident", ident))
        self.r512 = [k.sb("r512", [1, 512], F32) for _ in range(4)]
        self.r512_b = [Bf() for _ in range(4)]
        self.bb512 = [k.sb("bb512", [1, 512], BF16) for _ in range(2)]
        self.bb512_b = [Bf(), Bf()]
        self.P = [k.ps(f"P{i}", [128, 1024]) for i in range(3)]
        self.pb = [Bf() for _ in range(6)]
        self.PT = k.ps("PT", [128, 1024], BF16)
        self.pt_b = Bf()
        self.bt = [k.sb("bt", [128, 512], BF16) for _ in range(4)]
        self.bt_b = [Bf() for _ in range(4)]
        self.t512 = [k.sb("t512", [1, 512], F32) for _ in range(2)]
        self.t512_b = [Bf(), Bf()]
        self.b512 = [k.sb("b512", [1, 512], BF16) for _ in range(2)]
        self.b512_b = [Bf(), Bf()]
        self.s11 = k.sb("s11", [1, 1], F32)
        self.s11_b = Bf()

    def psum(self, i):
        return self.P[i // 2][:, 512 * (i % 2):512 * (i % 2 + 1)], self.pb[i]

    def psum2(self, i):
        return self.P[i][:], [self.pb[2 * i], self.pb[2 * i + 1]]

    def bias_cols(self, b_ap, n, tag):
        k = self.k
        g = n // 128
        rowsb = k.sb(tag + "r", [g, 128], BF16)
        rb = Bf()
        do(k.qp, "dma_start", [rb], [], out=rowsb[:],
           in_=b_ap.rearrange("(g j) -> g j", j=128))
        do(k.pe, "transpose", [self.pt_b], [rb, self.ident_b],
           self.PT[:, 0:g], rowsb[:], self.identsb[0:g, 0:g])
        colsb = k.sb(tag + "c", [128, g], F32)
        cb = Bf()
        do(k.dve, "tensor_copy", [cb], [self.pt_b], colsb[:], self.PT[:, 0:g])
        return colsb, cb

    def head_bias_cols(self, b_ap, off, tag):
        k = self.k
        rowsb = k.sb(tag + "r", [16, 64], BF16)
        rb = Bf()
        do(k.qp, "dma_start", [rb], [], out=rowsb[:],
           in_=b_ap[off:off + 1024].rearrange("(h d) -> h d", d=64))
        do(k.pe, "transpose", [self.pt_b], [rb, self.ident_b],
           self.PT[0:64, 0:16], rowsb[:], self.identsb[0:16, 0:16])
        colsb = k.sb(tag + "c", [64, 16], F32)
        cb = Bf()
        do(k.dve, "tensor_copy", [cb], [self.pt_b], colsb[:], self.PT[0:64, 0:16])
        return colsb, cb

    def bias_row(self, b_ap, n, tag):
        k = self.k
        rsb = k.sb(tag, [1, n], BF16)
        rb = Bf()
        do(k.qp, "dma_start", [rb], [], out=rsb[:],
           in_=b_ap.rearrange("(a n) -> a n", a=1))
        return rsb, rb


def stream_load(k, dst_sb, dst_b, src_fm):
    for c in range(NCH):
        do(k.qp, "dma_start", [dst_b], [], out=dst_sb[:, c * T:(c + 1) * T],
           in_=src_fm[c * 128:(c + 1) * 128, :])


def stream_store(k, src_sb, src_b, dst_fm):
    done = Bf()
    for c in range(NCH):
        do(k.qp, "dma_start", [done], [src_b], out=dst_fm[c * 128:(c + 1) * 128, :],
           in_=src_sb[:, c * T:(c + 1) * T])
    return done


def sbuf_provider(x_sb, x_b):
    xv = x_sb.ap().rearrange("p (c t) -> p c t", c=NCH)

    def get(lo, n):
        return xv[:, :, lo:lo + n], x_b
    return get


def dram_provider(k, sh, act_dram, act_b, tag):
    stg = [k.sb(tag, [128, NCH * 512], BF16) for _ in range(2)]
    stg_b = [Bf(), Bf()]
    state = {"i": 0, "cur": None}

    def get(lo, n):
        key = (lo, n)
        if state["cur"] == key:
            i = state["i"]
            return stg[i].ap().rearrange("p (c t) -> p c t", c=NCH)[:, :, 0:n], stg_b[i]
        i = 1 - state["i"]
        state["i"] = i
        state["cur"] = key
        v = stg[i].ap().rearrange("p (c t) -> p c t", c=NCH)
        do(k.qs, "dma_start", [stg_b[i]], [act_b], out=v[:, :, 0:n],
           in_=act_dram[:, :, lo:lo + n])
        return v[:, :, 0:n], stg_b[i]
    return get



def emit_attention(k, sh, get_band, cfg, proj_evac):
    wq, wk, wv, wo = cfg["wq"], cfg["wk"], cfg["wv"], cfg["wo"]
    rope = cfg.get("rope")
    scale_q = cfg["scale_q"]
    b_row = cfg.get("b_row")
    bqk = cfg.get("bqk_cols")

    wqkv = [k.sb("wqkv", [128, 3 * NCH * HD], BF16) for _ in range(2)]
    wqkv_b = [Bf(), Bf()]
    Qa = k.sb("Qa", [HD1, T], BF16); Qa_b = Bf()
    Ka = k.sb("Ka", [HD1, T], BF16); Ka_b = Bf()
    Va = k.sb("Va", [128, NT * HD1], BF16); Va_b = Bf()
    Ph = k.sb("Ph", [128, 1024], BF16)
    Ph_b = [Bf(), Bf()]
    sqb = k.sb("sqb", [HD, 512], BF16); sqb_b = Bf()
    nk2p = k.sb("nk2p", [1, NB], F32); nk2p_b = Bf()
    kmax2 = sh.s11; kmax2_b = sh.s11_b
    lrow = sh.t512[0]; lrow_b = sh.t512_b[0]
    rrow = sh.t512[1]; rrow_b = sh.t512_b[1]
    lrow_bf = sh.b512[0]; lrow_bf_b = sh.b512_b[0]
    rrow_bf = sh.b512[1]; rrow_bf_b = sh.b512_b[1]
    bc_sb = sh.bt[0]; bc_sb_b = sh.bt_b[0]
    ybuf = k.sb("ybuf", [HD, T], BF16); ybuf_b = Bf()
    yband = k.sb("yband", [128, NCH * 512], BF16); yband_b = Bf()
    wo_sb = k.sb("wosb", [128, NCH * D], BF16); wo_b = Bf()

    if rope:
        cosQ = k.sb("cosQ", [HD, T], BF16)
        sinQ = k.sb("sinQ", [HD, T], BF16)
        cosK = k.sb("cosK", [HD, T], BF16)
        sinK = k.sb("sinK", [HD, T], BF16)
        cs_b = Bf()
        for dst_, src_ in zip((cosQ, sinQ, cosK, sinK), rope):
            do(k.qp, "dma_start", [cs_b], [], out=dst_[:], in_=src_)
        rotT = np.zeros((HD, HD), dtype=ml_dtypes.bfloat16)
        for d in range(32):
            rotT[d + 32, d] = -1.0
            rotT[d, d + 32] = 1.0
        rot_sb = k.sb("rotsb", [HD, HD], BF16); rot_b = Bf()
        do(k.qp, "dma_start", [rot_b], [], out=rot_sb[:], in_=k.const("c_rotT", rotT))
        t1 = sh.bt[1]; t1_b = sh.bt_b[1]
        t2 = sh.bt[2]; t2_b = sh.bt_b[2]

    y_dram = k.dram(k.name("ydram"), [128, NCH, T], BF16)
    y_dram_b = Bf()

    s_ps, s_bb = sh.psum2(0)
    s_half = [s_ps[:, 0:512], s_ps[:, 512:1024]]
    qv_ps, qv_b = sh.psum(2)
    avf, av_b = sh.psum(3)
    av_ps = sh.P[1][0:HD1, 512:1024]
    bc_ps, bc_b = sh.psum(4)
    if rope:
        rt_full, rt_b = sh.psum(5)
        rt_ps = sh.P[2][0:HD, 512:1024]

    do(k.pool, "memset", [Ka_b], [], Ka[HD:HD1, :], -1.0)
    Va3 = Va.ap().rearrange("p (s e) -> p s e", e=HD1)
    do(k.pool, "memset", [Va_b], [], Va3[:, :, HD:HD1], 1.0)

    def load_head_w(h, slot):
        dst = wqkv[slot].ap().rearrange("p (i c n) -> p i c n", i=3, c=NCH)
        for i, w in enumerate((wq, wk, wv)):
            do(k.qp, "dma_start", [wqkv_b[slot]], [], out=dst[:, i],
               in_=w[:, HD * h: HD * (h + 1)].rearrange("(c p) n -> p c n", p=128))
        if cfg.get("w_scale_cols") is not None:
            gcol, gcol_b = cfg["w_scale_cols"]
            for i in range(3):
                for c in range(NCH):
                    do(k.dve, "tensor_scalar", [wqkv_b[slot]], [gcol_b],
                       dst[:, i, c, :], dst[:, i, c, :],
                       gcol[:, c:c + 1], None, ALU.mult)

    do(k.qp, "dma_start", [wo_b], [], out=wo_sb[:],
       in_=wo.rearrange("(c p) n -> p c n", p=128))
    load_head_w(0, 0)

    for h in range(HEADS):
        slot = h % 2
        wb = wqkv_b[slot]
        wv3 = wqkv[slot].ap().rearrange("p (i c n) -> p i c n", i=3, c=NCH)
        if h + 1 < HEADS:
            load_head_w(h + 1, 1 - slot)

        for band in range(NB):
            sl = slice(512 * band, 512 * (band + 1))
            xb_ap, xb_b = get_band(512 * band, 512)
            for i in range(2):
                dstbuf, dstb = ((Qa, Qa_b), (Ka, Ka_b))[i]
                for c in range(NCH):
                    do(k.pe, "matmul", [qv_b], [wb, xb_b],
                       qv_ps[0:HD, :], wv3[:, i, c, :], xb_ap[:, c, :],
                       start=(c == 0), stop=(c == NCH - 1), relaxed=(c != 0))
                if rope:
                    cos, sin = (cosQ, sinQ) if i == 0 else (cosK, sinK)
                    do(k.dve, "tensor_copy", [t1_b], [qv_b], t1[0:HD, :], qv_ps[0:HD, :])
                    do(k.pe, "matmul", [rt_b], [rot_b, t1_b],
                       rt_ps, rot_sb[:], t1[0:HD, :], start=True, stop=True)
                    do(k.dve, "tensor_tensor", [t2_b], [t1_b, cs_b],
                       t2[0:HD, :], t1[0:HD, :], cos[:, sl], ALU.mult)
                    do(k.dve, "tensor_tensor", [t1_b], [rt_b, cs_b],
                       t1[0:HD, :], rt_ps, sin[:, sl], ALU.mult)
                    do(k.dve, "tensor_tensor", [dstb], [t1_b, t2_b],
                       dstbuf[0:HD, sl], t1[0:HD, :], t2[0:HD, :], ALU.add)
                elif i == 0:
                    if bqk is not None:
                        do(k.dve, "tensor_scalar", [dstb], [qv_b, bqk[1][1]],
                           dstbuf[0:HD, sl], qv_ps[0:HD, :],
                           bqk[0][0][:, h:h + 1], scale_q, ALU.add, ALU.mult)
                    else:
                        do(k.dve, "tensor_scalar", [dstb], [qv_b],
                           dstbuf[0:HD, sl], qv_ps[0:HD, :], scale_q, None, ALU.mult)
                else:
                    if bqk is not None:
                        do(k.dve, "tensor_scalar", [dstb], [qv_b, bqk[1][1]],
                           dstbuf[0:HD, sl], qv_ps[0:HD, :],
                           bqk[1][0][:, h:h + 1], None, ALU.add)
                    else:
                        do(k.dve, "tensor_copy", [dstb], [qv_b],
                           dstbuf[0:HD, sl], qv_ps[0:HD, :])
            for s in range(4 * band, 4 * band + 4):
                for c in range(NCH):
                    do(k.pe, "matmul", [qv_b], [wb, xb_b],
                       qv_ps[:, 0:HD], xb_ap[:, c, 128 * (s - 4 * band):128 * (s - 4 * band) + 128],
                       wv3[:, 2, c, :], start=(c == 0), stop=(c == NCH - 1), relaxed=(c != 0))
                do(k.dve, "tensor_copy", [Va_b], [qv_b], Va3[:, s, 0:HD], qv_ps[:, 0:HD])

        for band in range(NB):
            bsl = slice(512 * band, 512 * (band + 1))
            do(k.act, "activation", [sqb_b], [Ka_b], out=sqb[:], in_=Ka[0:HD, bsl],
               func=AF.Square)
            do(k.pe, "matmul", [bc_b], [sqb_b, sh.ones_col_b],
               bc_ps[0:1, :], sh.ones_col[0:HD, :], sqb[:], start=True, stop=True)
            do(k.dve, "tensor_reduce", [nk2p_b], [bc_b], nk2p[:, band:band + 1],
               bc_ps[0:1, :], axis=AX, op=ALU.max)
        do(k.dve, "reduce_max", [kmax2_b], [nk2p_b], kmax2[:], nk2p[:], axis=AX)
        for band in range(NB):
            bsl = slice(512 * band, 512 * (band + 1))
            do(k.act, "activation", [sqb_b], [Qa_b], out=sqb[:], in_=Qa[0:HD, bsl],
               func=AF.Square)
            do(k.pe, "matmul", [bc_b], [sqb_b, sh.ones_col_b],
               bc_ps[0:1, :], sh.ones_col[0:HD, :], sqb[:], start=True, stop=True)
            do(k.dve, "tensor_scalar", [rrow_b], [bc_b, kmax2_b],
               rrow[:], bc_ps[0:1, :], kmax2[:], 0.5, ALU.add, ALU.mult)
            do(k.qp, "dma_start", [Qa_b], [rrow_b], out=Qa[HD:HD1, bsl], in_=rrow[:])

        for band in range(NB):
            ns = 4 * (band + 1)
            for st in range(ns):
                half = st % 2
                do(k.pe, "matmul", [s_bb[half]], [Qa_b, Ka_b],
                   s_half[half], Ka[:, 128 * st:128 * (st + 1)],
                   Qa[:, 512 * band:512 * (band + 1)], start=True, stop=True)
                do(k.act, "activation", [Ph_b[half]], [s_bb[half]],
                   out=Ph[:, 512 * half:512 * (half + 1)], in_=s_half[half], func=AF.Exp)
                j = st - 4 * band
                if j >= 0:
                    do(k.pool, "affine_select", [Ph_b[half]], [],
                       Ph[:, 512 * half:512 * (half + 1)],
                       Ph[:, 512 * half:512 * (half + 1)],
                       pattern=[[1, 512]], compare_op=ALU.is_ge, fill=0.0,
                       base=-128 * j, channel_multiplier=-1)
                do(k.pe, "matmul", [av_b], [Ph_b[half], Va_b],
                   av_ps, Va3[:, st, :], Ph[:, 512 * half:512 * (half + 1)],
                   start=(st == 0), stop=(st == ns - 1), relaxed=(st != 0))
            sl = slice(512 * band, 512 * (band + 1))
            do(k.dve, "tensor_copy", [lrow_b], [av_b], lrow[:], av_ps[HD:HD1, :])
            if b_row is not None:
                do(k.dve, "tensor_copy", [lrow_bf_b], [lrow_b], lrow_bf[:], lrow[:])
                do(k.pe, "matmul", [av_b], [lrow_bf_b, b_row[1]],
                   av_ps[0:HD, :], b_row[0][0:1, 2048 + HD * h:2048 + HD * (h + 1)],
                   lrow_bf[:], start=False, stop=True)
            do(k.dve, "reciprocal", [rrow_b], [lrow_b], rrow[:], lrow[:])
            do(k.dve, "tensor_copy", [rrow_bf_b], [rrow_b], rrow_bf[:], rrow[:])
            do(k.pe, "matmul", [bc_b], [rrow_bf_b, sh.ones_row_b],
               bc_ps[0:HD, :], sh.ones_pool[0:1, 0:HD], rrow_bf[:], start=True, stop=True)
            do(k.dve, "tensor_copy", [bc_sb_b], [bc_b], bc_sb[0:HD, :], bc_ps[0:HD, :])
            do(k.dve, "tensor_tensor", [ybuf_b], [bc_sb_b, av_b],
               ybuf[:, sl], av_ps[0:HD, :], bc_sb[0:HD, :], ALU.mult)
        do(k.qs, "dma_start", [y_dram_b], [ybuf_b],
           out=y_dram[64 * (h % 2):64 * (h % 2) + 64, h // 2, :], in_=ybuf[:])

    wo3 = wo_sb.ap().rearrange("p (c n) -> p c n", c=NCH)
    for band in range(NB):
        ybv = yband.ap().rearrange("p (c n) -> p c n", c=NCH)
        do(k.qs, "dma_start", [yband_b], [y_dram_b],
           out=ybv, in_=y_dram[:, :, 512 * band:512 * (band + 1)])
        for o in range(NCH):
            pslot = o % 2
            pps, ppb = sh.psum(0 + pslot)
            for c in range(NCH):
                do(k.pe, "matmul", [ppb], [wo_b, yband_b],
                   pps, wo3[:, c, 128 * o:128 * (o + 1)], ybv[:, c, :],
                   start=(c == 0), stop=(c == NCH - 1), relaxed=(c != 0))
            proj_evac(o, band, pps, ppb)



def emit_ln(k, sh, src_sb, src_b, dst_of, eps, tag, rms=False):
    sq = sh.bt[3]; sq_b = sh.bt_b[3]
    eps11 = k.sb(tag + "eps", [1, 1], F32); eps11_b = Bf()
    do(k.pool, "memset", [eps11_b], [], eps11[:], float(eps))
    st_ps, st_bb = sh.psum2(2)
    r = sh.r512
    rb = sh.r512_b
    mrs_bf, rstd_bf = sh.bb512
    mrs_bf_b, rstd_bf_b = sh.bb512_b
    bcM = sh.bt[0]; bcM_b = sh.bt_b[0]
    bcR = sh.bt[1]; bcR_b = sh.bt_b[1]
    tmp = sh.bt[2]; tmp_b = sh.bt_b[2]
    for band in range(NB):
        for c in range(NCH):
            sl = slice(c * T + 512 * band, c * T + 512 * (band + 1))
            if not rms:
                do(k.pe, "matmul", [st_bb[0]], [src_b, sh.ones_col_b],
                   st_ps[0:1, 0:512], sh.ones_col[:], src_sb[:, sl],
                   start=(c == 0), stop=(c == NCH - 1), relaxed=(c != 0))
            do(k.act, "activation", [sq_b], [src_b], out=sq[:],
               in_=src_sb[:, sl], func=AF.Square)
            do(k.pe, "matmul", [st_bb[1]], [sq_b, sh.ones_col_b],
               st_ps[0:1, 512:1024], sh.ones_col[:], sq[:],
               start=(c == 0), stop=(c == NCH - 1), relaxed=(c != 0))
        if not rms:
            do(k.dve, "tensor_scalar", [rb[0]], [st_bb[0]], r[0][:],
               st_ps[0:1, 0:512], 1.0 / D, None, ALU.mult)
            do(k.act, "activation", [rb[1]], [rb[0]], out=r[1][:], in_=r[0][:],
               func=AF.Square)
            do(k.dve, "scalar_tensor_tensor", [rb[2]], [st_bb[1], rb[1]], r[2][:],
               st_ps[0:1, 512:1024], 1.0 / D, r[1][:], ALU.mult, ALU.subtract)
        else:
            do(k.dve, "tensor_scalar", [rb[2]], [st_bb[1]], r[2][:],
               st_ps[0:1, 512:1024], 1.0 / D, None, ALU.mult)
        do(k.act, "activation", [rb[1]], [rb[2], eps11_b], out=r[1][:], in_=r[2][:],
           func=AF.Sqrt, bias=eps11[:])
        do(k.dve, "reciprocal", [rb[3]], [rb[1]], r[3][:], r[1][:])
        do(k.dve, "tensor_copy", [rstd_bf_b], [rb[3]], rstd_bf[:], r[3][:])
        do(k.pe, "matmul", [st_bb[1]], [rstd_bf_b, sh.ones_row_b],
           st_ps[:, 512:1024], sh.ones_pool[0:1, 0:128], rstd_bf[:],
           start=True, stop=True)
        do(k.dve, "tensor_copy", [bcR_b], [st_bb[1]], bcR[:], st_ps[:, 512:1024])
        if not rms:
            do(k.dve, "tensor_tensor", [mrs_bf_b], [rb[0], rb[3]], mrs_bf[:],
               r[0][:], r[3][:], ALU.mult)
            do(k.pe, "matmul", [st_bb[0]], [mrs_bf_b, sh.ones_row_b],
               st_ps[:, 0:512], sh.ones_pool[0:1, 0:128], mrs_bf[:],
               start=True, stop=True)
            do(k.dve, "tensor_copy", [bcM_b], [st_bb[0]], bcM[:], st_ps[:, 0:512])
        for c in range(NCH):
            sl = slice(c * T + 512 * band, c * T + 512 * (band + 1))
            dres = dst_of(c, band)
            dst_ap, dst_b = dres[0], dres[1]
            post = dres[2] if len(dres) > 2 else None
            if not rms:
                do(k.dve, "tensor_tensor", [tmp_b], [src_b, bcR_b],
                   tmp[:], src_sb[:, sl], bcR[:], ALU.mult)
                do(k.dve, "tensor_tensor", [dst_b], [tmp_b, bcM_b],
                   dst_ap, tmp[:], bcM[:], ALU.subtract)
            else:
                do(k.dve, "tensor_tensor", [dst_b], [src_b, bcR_b],
                   dst_ap, src_sb[:, sl], bcR[:], ALU.mult)
            if post is not None:
                post()


def emit_postln(k, sh, acc_sb, acc_b, eps=1e-5, tag="pln"):
    def dst_of(c, band):
        sl = slice(c * T + 512 * band, c * T + 512 * (band + 1))
        return acc_sb[:, sl], acc_b
    emit_ln(k, sh, acc_sb, acc_b, dst_of, eps, tag)


def dram_act_writer(k, sh, act_dram, act_b, tag):
    stg = [k.sb(tag, [128, NCH * 512], BF16) for _ in range(2)]
    stg_b = [Bf(), Bf()]

    def dst_of(c, band):
        i = band % 2
        v = stg[i].ap().rearrange("p (c t) -> p c t", c=NCH)
        post = None
        if c == NCH - 1:
            def post():
                do(k.qs, "dma_start", [act_b], [stg_b[i]],
                   out=act_dram[:, :, 512 * band:512 * (band + 1)], in_=v)
        return v[:, c, :], stg_b[i], post
    return dst_of


def emit_gpt_mlp(k, sh, xn_sb, xn_b, w_fc, b_fc, w_mlp, b_mlp, accum_fn):
    C2 = 0.7978845608028654 * 0.044715
    C1 = 0.7978845608028654
    bfc_cols, bfc_b = sh.bias_cols(b_fc, 4096, "bfc")
    bmlp_cols, bmlp_b = sh.bias_cols(b_mlp, D, "bmlp")
    wfc_sb = k.sb("wfc", [128, NCH * 2048], BF16); wfc_b = Bf()
    wml_sb = k.sb("wml", [128, 16 * D], BF16); wml_b = Bf()
    H = k.sb("Hb", [128, 16 * 512], BF16); H_b = Bf()
    hb = sh.bt[0]; hb_b = sh.bt_b[0]
    x2 = sh.bt[1]; x2_b = sh.bt_b[1]
    u1 = sh.bt[2]; u1_b = sh.bt_b[2]
    th = sh.bt[3]; th_b = sh.bt_b[3]
    stage = k.sb("mstg", [128, NCH * 512], BF16); stage_b = Bf()
    fc_ps, fc_bb = sh.psum2(0)
    dn_ps, dn_bb = sh.psum2(1)
    for half in range(2):
        do(k.qp, "dma_start", [wfc_b], [],
           out=wfc_sb.ap().rearrange("p (c n) -> p c n", c=NCH),
           in_=w_fc[:, 2048 * half:2048 * (half + 1)].rearrange("(c p) n -> p c n", p=128))
        do(k.qp, "dma_start", [wml_b], [],
           out=wml_sb.ap().rearrange("p (c n) -> p c n", c=16),
           in_=w_mlp[2048 * half:2048 * (half + 1), :].rearrange("(c p) n -> p c n", p=128))
        wfc3 = wfc_sb.ap().rearrange("p (c n) -> p c n", c=NCH)
        wml3 = wml_sb.ap().rearrange("p (c n) -> p c n", c=16)
        H3 = H.ap().rearrange("p (c n) -> p c n", c=16)
        for band in range(NB):
            for ht in range(16):
                pp = fc_ps[:, 512 * (ht % 2):512 * (ht % 2 + 1)]
                ppb = fc_bb[ht % 2]
                for c in range(NCH):
                    do(k.pe, "matmul", [ppb], [wfc_b, xn_b],
                       pp, wfc3[:, c, 128 * ht:128 * (ht + 1)],
                       xn_sb[:, c * T + 512 * band: c * T + 512 * (band + 1)],
                       start=(c == 0), stop=(c == NCH - 1), relaxed=(c != 0))
                bcol = bfc_cols[:, 16 * half + ht:16 * half + ht + 1]
                do(k.dve, "tensor_scalar", [hb_b], [ppb, bfc_b], hb[:], pp,
                   bcol, None, ALU.add)
                do(k.act, "activation", [x2_b], [hb_b], out=x2[:], in_=hb[:],
                   func=AF.Square)
                do(k.dve, "scalar_tensor_tensor", [u1_b], [x2_b, hb_b], u1[:],
                   x2[:], C2, hb[:], ALU.mult, ALU.mult)
                do(k.dve, "scalar_tensor_tensor", [x2_b], [hb_b, u1_b], x2[:],
                   hb[:], C1, u1[:], ALU.mult, ALU.add)
                do(k.act, "activation", [th_b], [x2_b], out=th[:], in_=x2[:],
                   func=AF.Tanh)
                do(k.dve, "scalar_tensor_tensor", [H_b], [th_b, hb_b],
                   H3[:, ht, :], th[:], 1.0, hb[:], ALU.add, ALU.mult)
            stg3 = stage.ap().rearrange("p (c n) -> p c n", c=NCH)
            for o in range(NCH):
                pp = dn_ps[:, 512 * (o % 2):512 * (o % 2 + 1)]
                ppb = dn_bb[o % 2]
                for hc in range(16):
                    do(k.pe, "matmul", [ppb], [wml_b, H_b],
                       pp, wml3[:, hc, 128 * o:128 * (o + 1)], H3[:, hc, :],
                       start=(hc == 0), stop=(hc == 15), relaxed=(hc != 0))
                if half == 0:
                    do(k.dve, "scalar_tensor_tensor", [stage_b], [ppb, bmlp_b],
                       stg3[:, o, :], pp, 0.5, None, ALU.mult, ALU.bypass) \
                        if False else \
                    do(k.dve, "tensor_scalar", [stage_b], [ppb, bmlp_b],
                       stg3[:, o, :], pp, 0.5, bmlp_cols[:, o:o + 1], ALU.mult, ALU.add)
                else:
                    do(k.dve, "tensor_scalar", [stage_b], [ppb],
                       stg3[:, o, :], pp, 0.5, None, ALU.mult)
            accum_fn(band, stage, stage_b, half == 0)


def accum_stream(k, dram3, dram_b, sb, sb_b, op=ALU.add):
    for c in range(NCH):
        for j in range(0, T, 2048):
            do(k.qp, "dma_start", [dram_b], [sb_b], out=dram3[:, c, j:j + 2048],
               in_=sb[:, c * T + j:c * T + j + 2048], accum_op=op)


def emit_routing_top2(k, sh, get_band, w_router, g2cols, combine_T, combine_T_b):
    wr_sb = k.sb("wr", [128, NCH * 8], BF16); wr_b = Bf()
    do(k.qp, "dma_start", [wr_b], [],
       out=wr_sb.ap().rearrange("p (c n) -> p c n", c=NCH),
       in_=w_router.rearrange("(c p) n -> p c n", p=128))
    wr3 = wr_sb.ap().rearrange("p (c n) -> p c n", c=NCH)
    for c in range(NCH):
        do(k.dve, "tensor_scalar", [wr_b], [g2cols[1]], wr3[:, c, :], wr3[:, c, :],
           g2cols[0][:, c:c + 1], None, ALU.mult)
    iota8 = k.const("c_iota8", np.tile(np.arange(8, dtype=np.float32), (128, 1)))
    iotaB = k.const("c_iotaB", np.tile(np.arange(8, dtype=np.float32) + 1000.0, (128, 1)))
    io8 = k.sb("io8", [128, 8], F32); io8_b = Bf()
    ioB = k.sb("ioB", [128, 8], F32); ioB_b = Bf()
    do(k.qp, "dma_start", [io8_b], [], out=io8[:], in_=iota8)
    do(k.qp, "dma_start", [ioB_b], [], out=ioB[:], in_=iotaB)
    lg = k.sb("lg", [128, 8], F32); lg_b = Bf()
    lg2 = k.sb("lg2", [128, 8], F32); lg2_b = Bf()
    oh1 = k.sb("oh1", [128, 8], F32); oh1_b = Bf()
    oh2 = k.sb("oh2", [128, 8], F32); oh2_b = Bf()
    cmb = k.sb("cmb", [128, 8], BF16); cmb_b = Bf()
    m1 = k.sb("m1", [128, 1], F32); m1_b = Bf()
    m2 = k.sb("m2c", [128, 1], F32); m2_b = Bf()
    i1 = k.sb("i1", [128, 1], F32); i1_b = Bf()
    p2u = k.sb("p2u", [128, 1], F32); p2u_b = Bf()
    rden = k.sb("rden", [128, 1], F32); rden_b = Bf()
    t8 = k.sb("t8", [128, 8], F32); t8_b = Bf()
    lg_ps, lg_pb = sh.psum(0)
    for band in range(NB):
        xb_ap, xb_b = get_band(512 * band, 512)
        for j in range(4):
            tile = 4 * band + j
            for c in range(NCH):
                do(k.pe, "matmul", [lg_pb], [xb_b, wr_b],
                   lg_ps[:, 0:8], xb_ap[:, c, 128 * j:128 * (j + 1)], wr3[:, c, :],
                   start=(c == 0), stop=(c == NCH - 1), relaxed=(c != 0))
            do(k.dve, "tensor_copy", [lg_b], [lg_pb], lg[:], lg_ps[:, 0:8])
            do(k.dve, "reduce_max", [m1_b], [lg_b], m1[:], lg[:], axis=AX)
            do(k.dve, "tensor_scalar", [oh1_b], [lg_b, m1_b], oh1[:], lg[:],
               m1[:], None, ALU.is_equal)
            do(k.dve, "scalar_tensor_tensor", [t8_b], [oh1_b, ioB_b], t8[:],
               oh1[:], -1000.0, ioB[:], ALU.mult, ALU.add)
            do(k.dve, "tensor_reduce", [i1_b], [t8_b], i1[:], t8[:], axis=AX, op=ALU.min)
            do(k.dve, "tensor_scalar", [oh1_b], [io8_b, i1_b], oh1[:], io8[:],
               i1[:], None, ALU.is_equal)
            do(k.dve, "scalar_tensor_tensor", [lg2_b], [oh1_b, lg_b], lg2[:],
               oh1[:], -1.0e30, lg[:], ALU.mult, ALU.add)
            do(k.dve, "reduce_max", [m2_b], [lg2_b], m2[:], lg2[:], axis=AX)
            do(k.dve, "tensor_scalar", [oh2_b], [lg2_b, m2_b], oh2[:], lg2[:],
               m2[:], None, ALU.is_equal)
            do(k.dve, "scalar_tensor_tensor", [t8_b], [oh2_b, ioB_b], t8[:],
               oh2[:], -1000.0, ioB[:], ALU.mult, ALU.add)
            do(k.dve, "tensor_reduce", [i1_b], [t8_b], i1[:], t8[:], axis=AX, op=ALU.min)
            do(k.dve, "tensor_scalar", [oh2_b], [io8_b, i1_b], oh2[:], io8[:],
               i1[:], None, ALU.is_equal)
            do(k.dve, "tensor_tensor", [rden_b], [m1_b, m2_b], rden[:], m2[:], m1[:],
               ALU.subtract)
            do(k.act, "activation", [p2u_b], [rden_b], out=p2u[:], in_=rden[:], func=AF.Exp)
            do(k.dve, "tensor_scalar", [rden_b], [p2u_b], rden[:], p2u[:], 1.0,
               None, ALU.add)
            do(k.dve, "reciprocal", [rden_b], [rden_b], rden[:], rden[:]) \
                if False else None
            do(k.dve, "reciprocal", [i1_b], [rden_b], i1[:], rden[:])
            do(k.dve, "tensor_scalar", [t8_b], [oh2_b, p2u_b], t8[:], oh2[:],
               p2u[:], None, ALU.mult)
            do(k.dve, "tensor_tensor", [lg2_b], [t8_b, oh1_b], lg2[:], t8[:],
               oh1[:], ALU.add)
            do(k.dve, "tensor_scalar", [cmb_b], [lg2_b, i1_b], cmb[:], lg2[:],
               i1[:], None, ALU.mult)
            do(k.pe, "transpose", [sh.pt_b], [cmb_b, sh.ident_b],
               sh.PT[0:8, 0:128], cmb[:], sh.identsb[:])
            do(k.dve, "tensor_copy", [combine_T_b], [sh.pt_b],
               combine_T[:, 128 * tile:128 * (tile + 1)], sh.PT[0:8, 0:128])


def emit_moe_l0(k, sh, get_band, combine_T, combine_T_b, wg, wu, wd, g2cols,
                moe3, moe_b):
    NHC = 2752 // 128
    assert 2752 % 128 == 64
    hts = [(i * 128, 128) for i in range(21)] + [(21 * 128, 64)]
    wg_sb = k.sb("wgsb", [128, NCH * 2752], BF16); wg_b = Bf()
    wu_sb = k.sb("wusb", [128, NCH * 2752], BF16); wu_b = Bf()
    wd_sb = k.sb("wdsb", [128, 22 * D], BF16); wd_b = Bf()
    GU = k.sb("GU", [128, 22 * 512], BF16); GU_b = Bf()
    coef = k.sb("coef", [128, 512], BF16); coef_b = Bf()
    btG = sh.bt[0]; btG_b = sh.bt_b[0]
    btU = sh.bt[1]; btU_b = sh.bt_b[1]
    dstg = [k.sb("dstg", [128, 512], BF16) for _ in range(2)]
    dstg_b = [Bf(), Bf()]
    sel = k.const("c_sel8", np.kron(np.eye(8), np.ones((1, 128))).astype(ml_dtypes.bfloat16))
    sel_sb = k.sb("sel8", [8, 8 * 128], BF16); sel_b = Bf()
    do(k.qp, "dma_start", [sel_b], [], out=sel_sb[:], in_=sel)
    up_ps, up_bb = sh.psum2(0)
    dn_ps, dn_bb = sh.psum2(1)
    cf_ps, cf_pb = sh.psum(4)
    wg3 = wg_sb.ap().rearrange("p (c n) -> p c n", c=NCH)
    wu3 = wu_sb.ap().rearrange("p (c n) -> p c n", c=NCH)
    wd3 = wd_sb.ap().rearrange("p (c n) -> p c n", c=22)
    GU3 = GU.ap().rearrange("p (c n) -> p c n", c=22)

    for e in range(8):
        do(k.qp, "dma_start", [wg_b], [], out=wg3,
           in_=wg[e].rearrange("(c p) n -> p c n", p=128))
        do(k.qp, "dma_start", [wu_b], [], out=wu3,
           in_=wu[e].rearrange("(c p) n -> p c n", p=128))
        for c in range(NCH):
            do(k.dve, "tensor_scalar", [wg_b], [g2cols[1]], wg3[:, c, :],
               wg3[:, c, :], g2cols[0][:, c:c + 1], None, ALU.mult)
            do(k.dve, "tensor_scalar", [wu_b], [g2cols[1]], wu3[:, c, :],
               wu3[:, c, :], g2cols[0][:, c:c + 1], None, ALU.mult)
        do(k.qp, "dma_start", [wd_b], [],
           out=wd_sb.ap().rearrange("p (c n) -> p c n", c=22)[:, 0:21],
           in_=wd[e][0:2688, :].rearrange("(c p) n -> p c n", p=128))
        do(k.qp, "dma_start", [wd_b], [], out=wd3[0:64, 21, :], in_=wd[e][2688:2752, :])
        for band in range(NB):
            xb_ap, xb_b = get_band(512 * band, 512)
            do(k.pe, "matmul", [cf_pb], [combine_T_b, sel_b],
               cf_ps, sel_sb[:, 128 * e:128 * (e + 1)],
               combine_T[:, 512 * band:512 * (band + 1)], start=True, stop=True)
            do(k.dve, "tensor_copy", [coef_b], [cf_pb], coef[:], cf_ps)
            for hi, (h0, hn_) in enumerate(hts):
                gp = up_ps[:, 0:512]
                up_ = up_ps[:, 512:1024]
                for c in range(NCH):
                    do(k.pe, "matmul", [up_bb[0]], [wg_b, xb_b],
                       gp[0:hn_, :], wg3[:, c, h0:h0 + hn_], xb_ap[:, c, :],
                       start=(c == 0), stop=(c == NCH - 1), relaxed=(c != 0))
                    do(k.pe, "matmul", [up_bb[1]], [wu_b, xb_b],
                       up_[0:hn_, :], wu3[:, c, h0:h0 + hn_], xb_ap[:, c, :],
                       start=(c == 0), stop=(c == NCH - 1), relaxed=(c != 0))
                do(k.act, "activation", [btG_b], [up_bb[0]], out=btG[0:hn_, :],
                   in_=gp[0:hn_, :], func=AF.Silu)
                do(k.dve, "tensor_tensor", [btU_b], [up_bb[1], coef_b],
                   btU[0:hn_, :], up_[0:hn_, :], coef[0:hn_, :], ALU.mult)
                do(k.dve, "tensor_tensor", [GU_b], [btG_b, btU_b],
                   GU3[0:hn_, hi, :], btG[0:hn_, :], btU[0:hn_, :], ALU.mult)
            for o in range(NCH):
                pp = dn_ps[:, 512 * (o % 2):512 * (o % 2 + 1)]
                ppb = dn_bb[o % 2]
                for hi, (h0, hn_) in enumerate(hts):
                    do(k.pe, "matmul", [ppb], [wd_b, GU_b],
                       pp, wd3[0:hn_, hi, 128 * o:128 * (o + 1)], GU3[0:hn_, hi, :],
                       start=(hi == 0), stop=(hi == 21), relaxed=(hi != 0))
                ds = o % 2
                do(k.dve, "tensor_copy", [dstg_b[ds]], [ppb], dstg[ds][:], pp)
                do(k.qp, "dma_start", [moe_b], [dstg_b[ds]],
                   out=moe3[:, o, 512 * band:512 * (band + 1)], in_=dstg[ds][:],
                   accum_op=(ALU.bypass if e == 0 else ALU.add))


def rope_tables(scale_q):
    inv_freq = 1.0 / (10000.0 ** (np.arange(0, HD, 2, dtype=np.float32) / HD))
    t = np.arange(T, dtype=np.float32)
    freqs = np.einsum('i,j->ij', t, inv_freq)
    emb = np.concatenate([freqs, freqs], axis=-1)
    cos = np.cos(emb).T.copy()
    sin = np.sin(emb).T.copy()
    bf = ml_dtypes.bfloat16
    return (np.ascontiguousarray((cos * scale_q).astype(bf)),
            np.ascontiguousarray((sin * scale_q).astype(bf)),
            np.ascontiguousarray(cos.astype(bf)),
            np.ascontiguousarray(sin.astype(bf)))


def emit_l0ffn_block(k, sh, xin3, xin_b, xout3, xout_b, w):
    h3 = k.dram(k.name("hdram"), [128, NCH, T], BF16); h3_b = Bf()
    hn3 = k.dram(k.name("hndram"), [128, NCH, T], BF16); hn3_b = Bf()
    moe3 = k.dram(k.name("moedram"), [128, NCH, T], BF16); moe3_b = Bf()
    xhat3 = k.dram(k.name("xhat"), [128, NCH, T], BF16); xhat3_b = Bf()
    combine_T = k.sb("cmbT", [8, T], BF16); combine_T_b = Bf()
    g1cols, g1_b = sh.bias_cols(w["g1"], D, "g1c")
    g2cols, g2_b = sh.bias_cols(w["g2"], D, "g2c")
    with k.scope():
        A = k.sb("A0f", [128, NCH * T], BF16); A_b = Bf()
        for c in range(NCH):
            do(k.qs, "dma_start", [A_b], [xin_b], out=A[:, c * T:(c + 1) * T],
               in_=xin3[:, c, :])
        with k.scope():
            emit_ln(k, sh, A, A_b, dram_act_writer(k, sh, xhat3, xhat3_b, "wx"),
                    1e-6, "rms1", rms=True)

        def proj_evac(o, band, pps, ppb):
            sl = slice(o * T + 512 * band, o * T + 512 * (band + 1))
            do(k.dve, "tensor_tensor", [A_b], [ppb], A[:, sl], pps, A[:, sl], ALU.add)

        with k.scope():
            cq, sq_, ck, sk = rope_tables(0.125)
            cfg = dict(wq=w["wq"], wk=w["wk"], wv=w["wv"], wo=w["wo"], scale_q=0.125,
                       rope=(k.const("c_cosQ", cq), k.const("c_sinQ", sq_),
                             k.const("c_cosK", ck), k.const("c_sinK", sk)),
                       w_scale_cols=(g1cols, g1_b))
            emit_attention(k, sh, dram_provider(k, sh, xhat3, xhat3_b, "xhp"), cfg,
                           proj_evac)
        for c in range(NCH):
            do(k.qp, "dma_start", [h3_b], [A_b], out=h3[:, c, :],
               in_=A[:, c * T:(c + 1) * T])
        with k.scope():
            emit_ln(k, sh, A, A_b, dram_act_writer(k, sh, hn3, hn3_b, "wh"),
                    1e-6, "rms2", rms=True)
    with k.scope():
        emit_routing_top2(k, sh, dram_provider(k, sh, hn3, hn3_b, "rt"),
                          w["w_router"], (g2cols, g2_b), combine_T, combine_T_b)
    with k.scope():
        emit_moe_l0(k, sh, dram_provider(k, sh, hn3, hn3_b, "mo"),
                    combine_T, combine_T_b, w["wg"], w["wu"], w["wd"],
                    (g2cols, g2_b), moe3, moe3_b)
    import os as _os
    _tap = _os.environ.get('TAP', '')
    if _tap == 'cmb':
        do(k.qp, "dma_start", [xout_b], [combine_T_b], out=xout3[0:8, 0, :],
           in_=combine_T[:])
        return
    if _tap:
        tapd = {'h': h3, 'hn': hn3, 'moe': moe3, 'xhat': xhat3}[_tap]
        tapb = {'h': h3_b, 'hn': hn3_b, 'moe': moe3_b, 'xhat': xhat3_b}[_tap]
        for c in range(NCH):
            do(k.qp, "dma_start", [xout_b], [tapb, combine_T_b], out=xout3[:, c, :],
               in_=tapd[:, c, :])
        return
    with k.scope():
        A = k.sb("A0f2", [128, NCH * T], BF16); A_b = Bf()
        stg = k.sb("fstg", [128, NCH * 512], BF16); stg_b = Bf()
        for band in range(NB):
            v = stg.ap().rearrange("p (c t) -> p c t", c=NCH)
            bsl = slice(512 * band, 512 * (band + 1))
            do(k.qs, "dma_start", [stg_b], [xin_b], out=v, in_=xin3[:, :, bsl])
            for c in range(NCH):
                sl = slice(c * T + 512 * band, c * T + 512 * (band + 1))
                do(k.dve, "tensor_copy", [A_b], [stg_b], A[:, sl], v[:, c, :])
            do(k.qs, "dma_start", [stg_b], [h3_b, A_b], out=v, in_=h3[:, :, bsl])
            for c in range(NCH):
                sl = slice(c * T + 512 * band, c * T + 512 * (band + 1))
                do(k.dve, "tensor_tensor", [A_b], [stg_b], A[:, sl], A[:, sl],
                   v[:, c, :], ALU.add)
            do(k.qs, "dma_start", [stg_b], [moe3_b, A_b], out=v, in_=moe3[:, :, bsl])
            for c in range(NCH):
                sl = slice(c * T + 512 * band, c * T + 512 * (band + 1))
                do(k.dve, "tensor_tensor", [A_b], [stg_b], A[:, sl], A[:, sl],
                   v[:, c, :], ALU.add)
        emit_postln(k, sh, A, A_b, eps=1e-5, tag="plf")
        for c in range(NCH):
            do(k.qp, "dma_start", [xout_b], [A_b], out=xout3[:, c, :],
               in_=A[:, c * T:(c + 1) * T])


def emit_l1ffn_block(k, sh, xfull, xfull_b, xown, xown_b, prevmask, xout3, xout_b, w):
    NE = 8
    CAP = 4096.0
    moe3 = k.dram(k.name("moedr"), [128, NCH, T], BF16); moe3_b = Bf()
    combine_T = k.sb("cmbT1", [8, T], BF16); combine_T_b = Bf()
    triu = np.triu(np.ones((128, 128))).astype(np.float32)
    striu = np.triu(np.ones((128, 128)), 1).astype(np.float32)
    triu_c = k.const("c_triu", triu)
    striu_c = k.const("c_striu", striu)
    iota8 = k.const("c_iota8", np.tile(np.arange(8, dtype=np.float32), (128, 1)))
    iotaB = k.const("c_iotaB", np.tile(np.arange(8, dtype=np.float32) + 1000.0, (128, 1)))
    scratch = k.dram(k.name("rscratch"), [256, 8], F32); scratch_b = Bf()

    with k.scope():
        wg_sb = k.sb("wgl1", [128, NCH * 8], BF16); wg_b = Bf()
        do(k.qp, "dma_start", [wg_b], [],
           out=wg_sb.ap().rearrange("p (c n) -> p c n", c=NCH),
           in_=w["Wg"].rearrange("(c p) n -> p c n", p=128))
        wr3 = wg_sb.ap().rearrange("p (c n) -> p c n", c=NCH)
        trs = k.sb("trs", [128, 128], F32); trs_b = Bf()
        strs = k.sb("strs", [128, 128], F32); strs_b = Bf()
        do(k.qp, "dma_start", [trs_b], [], out=trs[:], in_=triu_c)
        do(k.qp, "dma_start", [strs_b], [], out=strs[:], in_=striu_c)
        io8 = k.sb("io8b", [128, 8], F32); io8_b = Bf()
        ioB = k.sb("ioBb", [128, 8], F32); ioB_b = Bf()
        do(k.qp, "dma_start", [io8_b], [], out=io8[:], in_=iota8)
        do(k.qp, "dma_start", [ioB_b], [], out=ioB[:], in_=iotaB)
        OH = k.sb("OHown", [128, NT * 8], F32); OH_b = Bf()
        OH3 = OH.ap().rearrange("p (t e) -> p t e", e=8)
        g1col = k.sb("g1own", [128, NT], F32); g1col_b = Bf()
        tcnt = k.sb("tcnt", [1, 2048], F32); tcnt_b = Bf()
        octile = k.sb("octile", [1, 256], F32); octile_b = Bf()
        pvmA = k.sb("pvmA", [4, 1], F32)
        pvmB = k.sb("pvmB", [4, 1], F32)
        pvm_b = Bf()
        do(k.qp, "dma_start", [pvm_b], [], out=pvmA[:],
           in_=prevmask[0:4].rearrange("(r a) -> r a", a=1))
        do(k.qp, "dma_start", [pvm_b], [], out=pvmB[:],
           in_=prevmask[4:8].rearrange("(r a) -> r a", a=1))
        lg = k.sb("lgl", [128, 8], F32); lg_b = Bf()
        m1 = k.sb("m1l", [128, 1], F32); m1_b = Bf()
        nm1 = k.sb("nm1l", [128, 1], F32); nm1_b = Bf()
        i1 = k.sb("i1l", [128, 1], F32); i1_b = Bf()
        t8 = k.sb("t8l", [128, 8], F32); t8_b = Bf()
        oh = k.sb("ohl", [128, 8], F32); oh_b = Bf()
        es = k.sb("esl", [128, 8], F32); es_b = Bf()
        zs = k.sb("zsl", [128, 1], F32); zs_b = Bf()
        stg = [k.sb("rstg", [128, NCH * 512], BF16) for _ in range(2)]
        stg_b = [Bf(), Bf()]
        lg_ps, lg_pb = sh.psum(0)
        ct_ps, ct_pb = sh.psum(2)
        def routing_pass(src_of, own):
            for rr in range(1 if own else NE):
                for band in range(NB):
                    si = band % 2
                    v = stg[si].ap().rearrange("p (c t) -> p c t", c=NCH)
                    do(k.qp, "dma_start", [stg_b[si]], [], out=v,
                       in_=src_of(rr, band))
                    for j in range(4):
                        tile = 4 * band + j
                        g = 32 * rr + tile
                        for c in range(NCH):
                            do(k.pe, "matmul", [lg_pb], [stg_b[si], wg_b],
                               lg_ps[:, 0:8], v[:, c, 128 * j:128 * (j + 1)], wr3[:, c, :],
                               start=(c == 0), stop=(c == NCH - 1), relaxed=(c != 0))
                        do(k.dve, "tensor_copy", [lg_b], [lg_pb], lg[:], lg_ps[:, 0:8])
                        do(k.dve, "reduce_max", [m1_b], [lg_b], m1[:], lg[:], axis=AX)
                        do(k.dve, "tensor_scalar", [oh_b], [lg_b, m1_b], oh[:], lg[:],
                           m1[:], None, ALU.is_equal)
                        do(k.dve, "scalar_tensor_tensor", [t8_b], [oh_b, ioB_b], t8[:],
                           oh[:], -1000.0, ioB[:], ALU.mult, ALU.add)
                        do(k.dve, "tensor_reduce", [i1_b], [t8_b], i1[:], t8[:],
                           axis=AX, op=ALU.min)
                        do(k.dve, "tensor_scalar", [oh_b], [io8_b, i1_b], oh[:], io8[:],
                           i1[:], None, ALU.is_equal)
                        do(k.pe, "matmul", [ct_pb], [oh_b, trs_b],
                           ct_ps[0:1, 0:8], trs[:, 127:128], oh[:], start=True, stop=True)
                        if not own:
                            do(k.dve, "tensor_copy", [tcnt_b], [ct_pb],
                               tcnt[0:1, 8 * g:8 * g + 8], ct_ps[0:1, 0:8])
                        else:
                            do(k.dve, "tensor_copy", [octile_b], [ct_pb],
                               octile[0:1, 8 * tile:8 * tile + 8], ct_ps[0:1, 0:8])
                            do(k.dve, "tensor_copy", [OH_b], [oh_b], OH3[:, tile, :], oh[:])
                            do(k.dve, "reduce_max", [nm1_b], [lg_b], nm1[:], lg[:],
                               axis=AX, negate=True)
                            do(k.act, "activation", [es_b], [lg_b, nm1_b], out=es[:],
                               in_=lg[:], func=AF.Exp, bias=nm1[:])
                            do(k.dve, "reduce_sum", [zs_b], [es_b], zs[:], es[:], axis=AX)
                            do(k.dve, "reciprocal", [g1col_b], [zs_b],
                               g1col[:, tile:tile + 1], zs[:])
        routing_pass(lambda rr, band: xfull[rr, :, 512 * band:512 * (band + 1)]
                     .rearrange("(c p) t -> p c t", p=128), False)
        routing_pass(lambda rr, band: xown[:, 512 * band:512 * (band + 1)]
                     .rearrange("(c p) t -> p c t", p=128), True)
        do(k.qp, "dma_start", [scratch_b], [tcnt_b],
           out=scratch.rearrange("(a g) e -> a (g e)", a=1), in_=tcnt[:])
        cnt2 = k.sb("cnt2", [128, 16], F32); cnt2_b = Bf()
        do(k.qp, "dma_start", [cnt2_b], [scratch_b],
           out=cnt2.ap().rearrange("p (a e) -> p a e", a=2),
           in_=scratch.rearrange("(a p) e -> p a e", p=128))
        rsel = np.zeros((128, 8), np.float32)
        for gg in range(128):
            rsel[gg, gg // 32] = 1.0
        rsel_c = k.const("c_rsel", rsel)
        rsel_sb = k.sb("rsel", [128, 8], F32); rsel_sb_b = Bf()
        do(k.qp, "dma_start", [rsel_sb_b], [], out=rsel_sb[:], in_=rsel_c)
        pf_ps, pf_pb = sh.psum(0)
        do(k.pe, "matmul", [pf_pb], [rsel_sb_b, cnt2_b],
           pf_ps[0:8, 0:8], rsel_sb[:, 0:8], cnt2[:, 0:8], start=True, stop=True)
        do(k.pe, "matmul", [pf_pb], [rsel_sb_b, cnt2_b],
           pf_ps[0:8, 8:16], rsel_sb[:, 0:8], cnt2[:, 8:16], start=True, stop=True)
        rankoff = k.sb("rankoff", [1, 8], F32); rankoff_b = Bf()
        ro_ps, ro_pb = sh.psum(2)
        rc2 = k.sb("rc2", [4, 16], F32); rc2_b = Bf()
        do(k.dve, "tensor_copy", [rc2_b], [pf_pb], rc2[:, :], pf_ps[0:4, 0:16])
        do(k.pe, "matmul", [ro_pb], [pvm_b, rc2_b],
           ro_ps[0:1, 0:8], pvmA[:], rc2[0:4, 0:8], start=True, stop=False)
        do(k.pe, "matmul", [ro_pb], [pvm_b, rc2_b],
           ro_ps[0:1, 0:8], pvmB[:], rc2[0:4, 8:16], start=False, stop=True)
        do(k.dve, "tensor_copy", [rankoff_b], [ro_pb], rankoff[:], ro_ps[0:1, 0:8])
        do(k.qp, "dma_start", [scratch_b], [octile_b],
           out=scratch[0:32, :].rearrange("(a g) e -> a (g e)", a=1), in_=octile[:])
        oc2 = k.sb("oc2", [32, 8], F32); oc2_b = Bf()
        do(k.qp, "dma_start", [oc2_b], [scratch_b], out=oc2[:], in_=scratch[0:32, :])
        do(k.pe, "matmul", [pf_pb], [strs_b, oc2_b],
           pf_ps[0:32, 0:8], strs[0:32, 0:32], oc2[:], start=True, stop=False)
        do(k.pe, "matmul", [pf_pb], [rankoff_b, trs_b],
           pf_ps[0:32, 0:8], trs[0:1, 0:32], rankoff[:], start=False, stop=True)
        pref = k.sb("pref", [32, 8], F32); pref_b = Bf()
        do(k.dve, "tensor_copy", [pref_b], [pf_pb], pref[:], pf_ps[0:32, 0:8])
        do(k.qp, "dma_start", [scratch_b], [pref_b], out=scratch[0:32, :], in_=pref[:])
        tbase = k.sb("tbase", [1, 256], F32); tbase_b = Bf()
        do(k.qp, "dma_start", [tbase_b], [scratch_b], out=tbase[:],
           in_=scratch[0:32, :].rearrange("(a g) e -> a (g e)", a=1))
        OHbf = OH
        p1_ps, p1_pb = sh.psum(2)
        pos = k.sb("pos", [128, 8], F32); pos_b = Bf()
        psel = k.sb("psel", [128, 1], F32); psel_b = Bf()
        keep = k.sb("keep", [128, 1], F32); keep_b = Bf()
        cmb = k.sb("cmb1", [128, 8], BF16); cmb_b = Bf()
        for tile in range(NT):
            do(k.pe, "matmul", [p1_pb], [trs_b, OH_b],
               p1_ps[:, 0:8], trs[:], OH3[:, tile, :], start=True, stop=False)
            do(k.pe, "matmul", [p1_pb], [tbase_b, trs_b],
               p1_ps[:, 0:8], trs[0:1, :], tbase[0:1, 8 * tile:8 * tile + 8],
               start=False, stop=True)
            do(k.dve, "tensor_copy", [pos_b], [p1_pb], pos[:], p1_ps[:, 0:8])
            do(k.dve, "tensor_tensor", [t8_b], [pos_b, OH_b], t8[:], pos[:],
               OH3[:, tile, :], ALU.mult)
            do(k.dve, "reduce_sum", [psel_b], [t8_b], psel[:], t8[:], axis=AX)
            do(k.dve, "tensor_scalar", [keep_b], [psel_b], keep[:], psel[:],
               CAP, None, ALU.is_le)
            do(k.dve, "tensor_tensor", [keep_b], [g1col_b], keep[:], keep[:],
               g1col[:, tile:tile + 1], ALU.mult)
            do(k.dve, "tensor_scalar", [cmb_b], [OH_b, keep_b], cmb[:],
               OH3[:, tile, :], keep[:], None, ALU.mult)
            do(k.pe, "transpose", [sh.pt_b], [cmb_b, sh.ident_b],
               sh.PT[0:8, 0:128], cmb[:], sh.identsb[:])
            do(k.dve, "tensor_copy", [combine_T_b], [sh.pt_b],
               combine_T[:, 128 * tile:128 * (tile + 1)], sh.PT[0:8, 0:128])
    with k.scope():
        sel = k.const("c_sel8", np.kron(np.eye(8), np.ones((1, 128))).astype(ml_dtypes.bfloat16))
        sel_sb = k.sb("sel81", [8, 8 * 128], BF16); sel_b = Bf()
        do(k.qp, "dma_start", [sel_b], [], out=sel_sb[:], in_=sel)
        W1h = k.sb("W1h", [128, NCH * 2048], BF16); W1_b = Bf()
        W2h = k.sb("W2h", [128, 16 * D], BF16); W2_b = Bf()
        H = k.sb("Hl1", [128, 16 * 512], BF16); H_b = Bf()
        coef = k.sb("coefl", [128, 512], BF16); coef_b = Bf()
        dstg = [k.sb("dsg1", [128, 512], BF16) for _ in range(2)]
        dstg_b = [Bf(), Bf()]
        stg = [k.sb("xstg1", [128, NCH * 512], BF16) for _ in range(2)]
        stg_b = [Bf(), Bf()]
        up_ps, up_bb = sh.psum2(0)
        dn_ps, dn_bb = sh.psum2(1)
        cf_ps, cf_pb = sh.psum(4)
        W13 = W1h.ap().rearrange("p (c n) -> p c n", c=NCH)
        W23 = W2h.ap().rearrange("p (c n) -> p c n", c=16)
        H3 = H.ap().rearrange("p (c n) -> p c n", c=16)
        for e in range(NE):
            b1cols, b1_b = sh.bias_cols(w["b1"][e], 4096, k.name("b1c"))
            b2cols, b2_b = sh.bias_cols(w["b2"][e], D, k.name("b2c"))
            for half in range(2):
                do(k.qp, "dma_start", [W1_b], [], out=W13,
                   in_=w["W1"][e][:, 2048 * half:2048 * (half + 1)].rearrange(
                       "(c p) n -> p c n", p=128))
                do(k.qp, "dma_start", [W2_b], [], out=W23,
                   in_=w["W2"][e][2048 * half:2048 * (half + 1), :].rearrange(
                       "(c p) n -> p c n", p=128))
                for band in range(NB):
                    si = band % 2
                    v = stg[si].ap().rearrange("p (c t) -> p c t", c=NCH)
                    do(k.qp, "dma_start", [stg_b[si]], [], out=v,
                       in_=xown[:, 512 * band:512 * (band + 1)].rearrange(
                           "(c p) t -> p c t", p=128))
                    do(k.pe, "matmul", [cf_pb], [combine_T_b, sel_b],
                       cf_ps, sel_sb[:, 128 * e:128 * (e + 1)],
                       combine_T[:, 512 * band:512 * (band + 1)], start=True, stop=True)
                    do(k.dve, "tensor_copy", [coef_b], [cf_pb], coef[:], cf_ps)
                    for ht in range(16):
                        pp = up_ps[:, 512 * (ht % 2):512 * (ht % 2 + 1)]
                        ppb = up_bb[ht % 2]
                        for c in range(NCH):
                            do(k.pe, "matmul", [ppb], [W1_b, stg_b[si]],
                               pp, W13[:, c, 128 * ht:128 * (ht + 1)], v[:, c, :],
                               start=(c == 0), stop=(c == NCH - 1), relaxed=(c != 0))
                        do(k.act, "activation", [H_b], [ppb, b1_b], out=H3[:, ht, :],
                           in_=pp, func=AF.Relu,
                           bias=b1cols[:, 16 * half + ht:16 * half + ht + 1])
                    for o in range(NCH):
                        pp = dn_ps[:, 512 * (o % 2):512 * (o % 2 + 1)]
                        ppb = dn_bb[o % 2]
                        for hc in range(16):
                            do(k.pe, "matmul", [ppb], [W2_b, H_b],
                               pp, W23[:, hc, 128 * o:128 * (o + 1)], H3[:, hc, :],
                               start=(hc == 0), stop=(hc == 15), relaxed=(hc != 0))
                        ds = o % 2
                        if half == 0:
                            do(k.dve, "scalar_tensor_tensor", [dstg_b[ds]],
                               [ppb, b2_b, coef_b], dstg[ds][:], pp,
                               b2cols[:, o:o + 1], coef[:], ALU.add, ALU.mult)
                        else:
                            do(k.dve, "tensor_tensor", [dstg_b[ds]], [ppb, coef_b],
                               dstg[ds][:], pp, coef[:], ALU.mult)
                        do(k.qp, "dma_start", [moe3_b], [dstg_b[ds]],
                           out=moe3[:, o, 512 * band:512 * (band + 1)],
                           in_=dstg[ds][:],
                           accum_op=(ALU.bypass if (e == 0 and half == 0) else ALU.add))
    with k.scope():
        A = k.sb("Al1f", [128, NCH * T], BF16); A_b = Bf()
        for c in range(NCH):
            do(k.qp, "dma_start", [A_b], [],
               out=A[:, c * T:(c + 1) * T],
               in_=xown[128 * c:128 * (c + 1), :])
        stg2 = k.sb("fstg1", [128, NCH * 512], BF16); stg2_b = Bf()
        for band in range(NB):
            v = stg2.ap().rearrange("p (c t) -> p c t", c=NCH)
            bsl = slice(512 * band, 512 * (band + 1))
            do(k.qs, "dma_start", [stg2_b], [moe3_b, A_b], out=v, in_=moe3[:, :, bsl])
            for c in range(NCH):
                sl = slice(c * T + 512 * band, c * T + 512 * (band + 1))
                do(k.dve, "tensor_tensor", [A_b], [stg2_b], A[:, sl], A[:, sl],
                   v[:, c, :], ALU.add)
        emit_postln(k, sh, A, A_b, eps=1e-5, tag="pl1f")
        for c in range(NCH):
            do(k.qp, "dma_start", [xout_b], [A_b], out=xout3[:, c, :],
               in_=A[:, c * T:(c + 1) * T])


from concourse.bass_utils import run_bass_kernel_spmd


def build_l0attn():
    k = K()
    x_in = k.inp("x", [D, T])
    w = dict(
        w_attn=k.inp("w_attn", [D, 3 * D]), b_attn=k.inp("b_attn", [3 * D]),
        w_proj=k.inp("w_proj", [D, D]), b_proj=k.inp("b_proj", [D]),
        w_fc=k.inp("w_fc", [D, 4 * D]), b_fc=k.inp("b_fc", [4 * D]),
        w_mlp=k.inp("w_mlp", [4 * D, D]), b_mlp=k.inp("b_mlp", [D]))
    out = k.out("o", [D, T])
    sh = Shared(k)
    A = k.sb("Astream", [128, NCH * T], BF16)
    A_b = Bf()
    stream_load(k, A, A_b, x_in)
    emit_l0attn_block(k, sh, A, A_b, w)
    done = stream_store(k, A, A_b, out)
    k.qs.wait(done.wev)
    return k


def emit_l0attn_block(k, sh, A, A_b, w):
    xacc = k.dram(k.name("xacc"), [128, NCH, T], BF16)
    xacc_b = Bf()
    for c in range(NCH):
        do(k.qp, "dma_start", [xacc_b], [A_b], out=xacc[:, c, :],
           in_=A[:, c * T:(c + 1) * T])
    xn1 = k.dram(k.name("xn1"), [128, NCH, T], BF16)
    xn1_b = Bf()
    with k.scope():
        emit_ln(k, sh, A, A_b, dram_act_writer(k, sh, xn1, xn1_b, "w1"),
                1e-10, "ln1")
    with k.scope():
        brow, brow_b = sh.bias_row(w["b_attn"], 3 * D, "battn")
        bq = sh.head_bias_cols(w["b_attn"], 0, "bq")
        bk = sh.head_bias_cols(w["b_attn"], 1024, "bk")
        bproj_cols, bproj_b = sh.bias_cols(w["b_proj"], D, "bpj")

        def proj_evac(o, band, pps, ppb):
            sl = slice(o * T + 512 * band, o * T + 512 * (band + 1))
            do(k.dve, "scalar_tensor_tensor", [A_b], [ppb, bproj_b],
               A[:, sl], pps, bproj_cols[:, o:o + 1], A[:, sl], ALU.add, ALU.add)

        cfg = dict(wq=w["w_attn"][:, 0:1024], wk=w["w_attn"][:, 1024:2048],
                   wv=w["w_attn"][:, 2048:3072], wo=w["w_proj"],
                   scale_q=0.125, b_row=(brow, brow_b), bqk_cols=(bq, bk))
        emit_attention(k, sh, dram_provider(k, sh, xn1, xn1_b, "xn1p"), cfg, proj_evac)
    accum_stream(k, xacc, xacc_b, A, A_b)
    with k.scope():
        def dst_of(c, band):
            sl = slice(c * T + 512 * band, c * T + 512 * (band + 1))
            return A[:, sl], A_b
        emit_ln(k, sh, A, A_b, dst_of, 1e-10, "ln2")
    with k.scope():
        def accum_fn(band, stage, stage_b, first):
            v = stage.ap().rearrange("p (c t) -> p c t", c=NCH)
            do(k.qp, "dma_start", [xacc_b], [stage_b],
               out=xacc[:, :, 512 * band:512 * (band + 1)], in_=v, accum_op=ALU.add)
        emit_gpt_mlp(k, sh, A, A_b, w["w_fc"], w["b_fc"], w["w_mlp"], w["b_mlp"],
                     accum_fn)
    for c in range(NCH):
        do(k.qs, "dma_start", [A_b], [xacc_b], out=A[:, c * T:(c + 1) * T],
           in_=xacc[:, c, :])
    emit_postln(k, sh, A, A_b, eps=1e-5, tag="pl0a")


def build_l0ffn():
    k = K()
    x_in = k.inp("x", [D, T])
    w = dict(
        wq=k.inp("wq", [D, D]), wk=k.inp("wk", [D, D]),
        wv=k.inp("wv", [D, D]), wo=k.inp("wo", [D, D]),
        w_router=k.inp("w_router", [D, 8]),
        wg=k.inp("wg", [8, D, 2752]), wu=k.inp("wu", [8, D, 2752]),
        wd=k.inp("wd", [8, 2752, D]),
        g1=k.inp("g1", [D]), g2=k.inp("g2", [D]))
    out = k.out("o", [D, T])
    sh = Shared(k)
    xin3 = k.dram("xin3", [128, NCH, T], BF16); xin_b = Bf()
    xout3 = k.dram("xout3", [128, NCH, T], BF16); xout_b = Bf()
    for c in range(NCH):
        do(k.qp, "dma_start", [xin_b], [], out=xin3[:, c, :],
           in_=x_in[c * 128:(c + 1) * 128, :])
    emit_l0ffn_block(k, sh, xin3, xin_b, xout3, xout_b, w)
    done = Bf()
    for c in range(NCH):
        do(k.qp, "dma_start", [done], [xout_b], out=out[c * 128:(c + 1) * 128, :],
           in_=xout3[:, c, :])
    k.qs.wait(done.wev)
    return k


def build_l1attn():
    k = K()
    x_in = k.inp("x", [D, T])
    wq = k.inp("wq", [D, D])
    wk = k.inp("wk", [D, D])
    wv = k.inp("wv", [D, D])
    wo = k.inp("wo", [D, D])
    out = k.out("o", [D, T])
    sh = Shared(k)
    x = k.sb("xstream", [128, NCH * T], BF16)
    x_b = Bf()
    stream_load(k, x, x_b, x_in)

    def proj_evac(o, band, pps, pj_b):
        sl = slice(o * T + 512 * band, o * T + 512 * (band + 1))
        do(k.dve, "tensor_tensor", [x_b], [pj_b], x[:, sl], pps, x[:, sl], ALU.add)

    cfg = dict(wq=wq, wk=wk, wv=wv, wo=wo, scale_q=0.125)
    with k.scope():
        emit_attention(k, sh, sbuf_provider(x, x_b), cfg, proj_evac)
    emit_postln(k, sh, x, x_b, eps=1e-5, tag="pl")
    done = stream_store(k, x, x_b, out)
    k.qs.wait(done.wev)
    return k


def build_l1ffn():
    k = K()
    xf = k.inp("xf", [8, D, T])
    xown = k.inp("xown", [D, T])
    pvm = k.inp("pvm", [8])
    w = dict(Wg=k.inp("Wg", [D, 8]), W1=k.inp("W1", [8, D, 4096]),
             b1=k.inp("b1", [8, 4096]), W2=k.inp("W2", [8, 4096, D]),
             b2=k.inp("b2", [8, D]))
    out = k.out("o", [D, T])
    sh = Shared(k)
    xout3 = k.dram("xout3", [128, NCH, T], BF16); xout_b = Bf()
    emit_l1ffn_block(k, sh, xf, Bf(), xown, Bf(), pvm, xout3, xout_b, w)
    done = Bf()
    for c in range(NCH):
        do(k.qp, "dma_start", [done], [xout_b], out=out[c * 128:(c + 1) * 128, :],
           in_=xout3[:, c, :])
    k.qs.wait(done.wev)
    return k


def _run(k, nc, per_core):
    maps = []
    for m in per_core:
        mm = dict(m)
        mm.update(k.consts)
        maps.append(mm)
    res = run_bass_kernel_spmd(nc, maps, core_ids=list(range(8)))
    return [res.results[c]["o"] for c in range(8)]


def kernel_4launch(**inputs):
    ct = np.ascontiguousarray
    xs = [ct(inputs["L0_attn_x"][c].T).astype(np.float32) for c in range(8)]

    k1 = build_l0attn(); nc1 = k1.finalize()
    w1 = dict(w_attn=ct(inputs["L0_attn_w_attn"].T), b_attn=inputs["L0_attn_b_attn"],
              w_proj=ct(inputs["L0_attn_w_proj"].T), b_proj=inputs["L0_attn_b_proj"],
              w_fc=ct(inputs["L0_attn_w_fc"].T), b_fc=inputs["L0_attn_b_fc"],
              w_mlp=ct(inputs["L0_attn_w_mlp_proj"].T), b_mlp=inputs["L0_attn_b_mlp_proj"])
    xs = _run(k1, nc1, [dict(w1, x=xs[c]) for c in range(8)])

    k2 = build_l0ffn(); nc2 = k2.finalize()
    w2 = dict(wq=inputs["L0_ffn_wq"], wk=inputs["L0_ffn_wk"], wv=inputs["L0_ffn_wv"],
              wo=inputs["L0_ffn_wo"], w_router=inputs["L0_ffn_w_router"],
              wg=inputs["L0_ffn_wg"], wu=inputs["L0_ffn_wu"], wd=inputs["L0_ffn_wd"],
              g1=inputs["L0_ffn_g1"], g2=inputs["L0_ffn_g2"])
    xs = _run(k2, nc2, [dict(w2, x=xs[c]) for c in range(8)])

    k3 = build_l1attn(); nc3 = k3.finalize()
    w3 = dict(wq=inputs["L1_attn_Wq"], wk=inputs["L1_attn_Wk"],
              wv=inputs["L1_attn_Wv"], wo=inputs["L1_attn_Wo"])
    xs = _run(k3, nc3, [dict(w3, x=xs[c]) for c in range(8)])

    k4 = build_l1ffn(); nc4 = k4.finalize()
    xf = np.stack(xs)
    w4 = dict(Wg=inputs["L1_ffn_Wg"], W1=inputs["L1_ffn_W1"], b1=inputs["L1_ffn_b1"],
              W2=inputs["L1_ffn_W2"], b2=inputs["L1_ffn_b2"])
    xs = _run(k4, nc4, [dict(w4, xf=xf, xown=xs[c],
                             pvm=(np.arange(8) < c).astype(np.float32))
                        for c in range(8)])
    return np.stack([xs[c].T for c in range(8)])


def emit_l1attn_block(k, sh, x, x_b, w3):
    def proj_evac(o, band, pps, pj_b):
        sl = slice(o * T + 512 * band, o * T + 512 * (band + 1))
        do(k.dve, "tensor_tensor", [x_b], [pj_b], x[:, sl], pps, x[:, sl], ALU.add)
    cfg = dict(wq=w3["wq"], wk=w3["wk"], wv=w3["wv"], wo=w3["wo"], scale_q=0.125)
    with k.scope():
        emit_attention(k, sh, sbuf_provider(x, x_b), cfg, proj_evac)
    emit_postln(k, sh, x, x_b, eps=1e-5, tag="pl")


def build_fusedA():
    k = K()
    x_in = k.inp("x", [D, T])
    w1 = dict(
        w_attn=k.inp("w_attn", [D, 3 * D]), b_attn=k.inp("b_attn", [3 * D]),
        w_proj=k.inp("w_proj", [D, D]), b_proj=k.inp("b_proj", [D]),
        w_fc=k.inp("w_fc", [D, 4 * D]), b_fc=k.inp("b_fc", [4 * D]),
        w_mlp=k.inp("w_mlp", [4 * D, D]), b_mlp=k.inp("b_mlp", [D]))
    w2 = dict(
        wq=k.inp("f_wq", [D, D]), wk=k.inp("f_wk", [D, D]),
        wv=k.inp("f_wv", [D, D]), wo=k.inp("f_wo", [D, D]),
        w_router=k.inp("w_router", [D, 8]),
        wg=k.inp("wg", [8, D, 2752]), wu=k.inp("wu", [8, D, 2752]),
        wd=k.inp("wd", [8, 2752, D]),
        g1=k.inp("g1", [D]), g2=k.inp("g2", [D]))
    w3 = dict(wq=k.inp("a_wq", [D, D]), wk=k.inp("a_wk", [D, D]),
              wv=k.inp("a_wv", [D, D]), wo=k.inp("a_wo", [D, D]))
    out = k.out("o", [D, T])
    sh = Shared(k)
    xin3 = k.dram("xin3", [128, NCH, T], BF16); xin_b = Bf()
    xout3 = k.dram("xout3", [128, NCH, T], BF16); xout_b = Bf()
    with k.scope():
        A = k.sb("AstreamA", [128, NCH * T], BF16)
        A_b = Bf()
        stream_load(k, A, A_b, x_in)
        emit_l0attn_block(k, sh, A, A_b, w1)
        for c in range(NCH):
            do(k.qp, "dma_start", [xin_b], [A_b], out=xin3[:, c, :],
               in_=A[:, c * T:(c + 1) * T])
    emit_l0ffn_block(k, sh, xin3, xin_b, xout3, xout_b, w2)
    with k.scope():
        A = k.sb("AstreamC", [128, NCH * T], BF16)
        A_b = Bf()
        for c in range(NCH):
            do(k.qs, "dma_start", [A_b], [xout_b], out=A[:, c * T:(c + 1) * T],
               in_=xout3[:, c, :])
        emit_l1attn_block(k, sh, A, A_b, w3)
        done = stream_store(k, A, A_b, out)
        k.qs.wait(done.wev)
    return k


def kernel(**inputs):
    ct = np.ascontiguousarray
    xs = [ct(inputs["L0_attn_x"][c].T).astype(np.float32) for c in range(8)]
    kA = build_fusedA(); ncA = kA.finalize()
    wA = dict(w_attn=ct(inputs["L0_attn_w_attn"].T), b_attn=inputs["L0_attn_b_attn"],
              w_proj=ct(inputs["L0_attn_w_proj"].T), b_proj=inputs["L0_attn_b_proj"],
              w_fc=ct(inputs["L0_attn_w_fc"].T), b_fc=inputs["L0_attn_b_fc"],
              w_mlp=ct(inputs["L0_attn_w_mlp_proj"].T), b_mlp=inputs["L0_attn_b_mlp_proj"],
              f_wq=inputs["L0_ffn_wq"], f_wk=inputs["L0_ffn_wk"],
              f_wv=inputs["L0_ffn_wv"], f_wo=inputs["L0_ffn_wo"],
              w_router=inputs["L0_ffn_w_router"], wg=inputs["L0_ffn_wg"],
              wu=inputs["L0_ffn_wu"], wd=inputs["L0_ffn_wd"],
              g1=inputs["L0_ffn_g1"], g2=inputs["L0_ffn_g2"],
              a_wq=inputs["L1_attn_Wq"], a_wk=inputs["L1_attn_Wk"],
              a_wv=inputs["L1_attn_Wv"], a_wo=inputs["L1_attn_Wo"])
    xs = _run(kA, ncA, [dict(wA, x=xs[c]) for c in range(8)])
    k4 = build_l1ffn(); nc4 = k4.finalize()
    xf = np.stack(xs)
    w4 = dict(Wg=inputs["L1_ffn_Wg"], W1=inputs["L1_ffn_W1"], b1=inputs["L1_ffn_b1"],
              W2=inputs["L1_ffn_W2"], b2=inputs["L1_ffn_b2"])
    xs = _run(k4, nc4, [dict(w4, xf=xf, xown=xs[c],
                             pvm=(np.arange(8) < c).astype(np.float32))
                        for c in range(8)])
    return np.stack([xs[c].T for c in range(8)])


# version 06 (final)
# speedup vs baseline: 1.0057x; 1.0057x over previous
import numpy as np
import ml_dtypes
from contextlib import ExitStack

import concourse.bass as bass
import concourse.mybir as mybir
from concourse.bass_core_rust import IndirectOffsetOnAxis

F32 = mybir.dt.float32
BF16 = mybir.dt.bfloat16
I32 = mybir.dt.int32
AF = mybir.ActivationFunctionType
ALU = mybir.AluOpType
AX = mybir.AxisListType.X

D = 1024
T = 4096
NCH = D // 128
NB = T // 512
NT = T // 128
HEADS = 16
HD = 64


def np2dt(d):
    d = np.dtype(d)
    if d == np.float32:
        return F32
    if d == np.int32:
        return I32
    if d == ml_dtypes.bfloat16:
        return BF16
    raise ValueError(d)


class Eng:

    def __init__(self, k, name, step, stream):
        self.k = k
        self.name = name
        self.step = step
        self.stream = stream
        self.n = 0
        self.waited = {}

    def wait(self, ev):
        if ev is None:
            return
        if isinstance(ev, list) or (isinstance(ev, tuple) and (not ev or isinstance(ev[0], (list, tuple)))):
            for e in ev:
                self.wait(e)
            return
        name, cnt, ep = ev
        if ep < self.k.epoch_id:
            return
        if cnt <= 0 or self.waited.get(name, 0) >= cnt:
            return
        self.waited[name] = cnt
        sem = self.k.sems[name]
        self.stream.append(lambda raw: raw.wait_ge(sem, cnt))

    def op(self, method, *args, **kwargs):
        if self.step == 16 and self.n > 0:
            self.wait((self.name, self.n, self.k.epoch_id))
        self.n += self.step
        n = self.n
        sem = self.k.sems[self.name]
        step = self.step

        def emit(raw):
            getattr(raw, method)(*args, **kwargs).then_inc(sem, step)
        self.stream.append(emit)
        return (self.name, n, self.k.epoch_id)

    def ev(self):
        return (self.name, self.n, self.k.epoch_id)


class K:

    def __init__(self):
        self.nc = bass.Bass("TRN2", target_bir_lowering=False)
        self.st = ExitStack()
        self.sems = {}
        self.streams = {"pe": [], "act": [], "dve": [], "pool": [], "sp": []}
        for name in ["pe", "act", "dve", "pool", "qs", "qp", "qa"]:
            self.sems[name] = self.st.enter_context(self.nc.semaphore(name))
        self.pe = Eng(self, "pe", 1, self.streams["pe"])
        self.act = Eng(self, "act", 1, self.streams["act"])
        self.dve = Eng(self, "dve", 1, self.streams["dve"])
        self.pool = Eng(self, "pool", 1, self.streams["pool"])
        self.qs = Eng(self, "qs", 16, self.streams["sp"])
        self.qp = Eng(self, "qp", 16, self.streams["pool"])
        self.qa = Eng(self, "qa", 16, self.streams["act"])
        self.consts = {}
        self._names = 0
        self.epoch_id = 0
        self.sems["epoch"] = self.st.enter_context(self.nc.semaphore("epoch"))

    def name(self, p):
        self._names += 1
        return f"{p}{self._names}"

    def sb(self, prefix, shape, dtype=BF16):
        return self.st.enter_context(self.nc.sbuf_tensor(self.name(prefix), shape, dtype))

    def ps(self, prefix, shape, dtype=F32):
        return self.st.enter_context(self.nc.psum_tensor(self.name(prefix), shape, dtype))

    def dram(self, nm, shape, dtype=BF16):
        return self.nc.dram_tensor(nm, shape, dtype, kind="Internal").ap()

    def inp(self, nm, shape, dtype=F32):
        return self.nc.dram_tensor(nm, list(shape), dtype, kind="ExternalInput").ap()

    def out(self, nm, shape, dtype=F32):
        return self.nc.dram_tensor(nm, list(shape), dtype, kind="ExternalOutput").ap()

    def const(self, nm, arr):
        if nm not in self.consts:
            a = np.ascontiguousarray(arr)
            self.consts[nm] = a
            self._cap = getattr(self, "_cap", {})
            self._cap[nm] = self.nc.dram_tensor(nm, list(a.shape), np2dt(a.dtype),
                                                kind="ExternalInput").ap()
        return self._cap[nm]

    def finalize(self):
        block = self.st.enter_context(self.nc.Block())

        def run(ops):
            def f(raw):
                for op in ops:
                    op(raw)
            return f
        block.tensor(run(self.streams["pe"]))
        block.scalar(run(self.streams["act"]))
        block.vector(run(self.streams["dve"]))
        block.gpsimd(run(self.streams["pool"]))
        block.sync(run(self.streams["sp"]))
        self.st.close()
        return self.nc


class Scope:
    def __init__(self, k):
        self.k = k
        self.st = ExitStack()

    def __enter__(self):
        self.k._scopes.append(self.st)
        return self

    def __exit__(self, *a):
        self.k._scopes.pop()
        self.k.barrier()
        self.st.close()
        return False


def _k_scope(self):
    return Scope(self)


def _k_epoch(self):
    self.barrier()
    engs = [self.pe, self.act, self.dve, self.pool, self.qs, self.qp, self.qa]
    epoch_sem = self.sems["epoch"]
    sems = [self.sems[nm] for nm in ["pe", "act", "dve", "pool", "qs", "qp", "qa"]]
    self.epoch_id += 1
    target = self.epoch_id

    def clear_all(raw):
        for s in sems:
            raw.sem_clear(s)
        raw.nop().then_inc(epoch_sem, 1)
    self.streams["pool"].append(clear_all)
    for st_name in ["pe", "act", "dve", "sp"]:
        self.streams[st_name].append(
            lambda raw, t=target: raw.wait_ge(epoch_sem, t))
    self.streams["pool"].append(
        lambda raw, t=target: raw.wait_ge(epoch_sem, t))
    for e in engs:
        e.n = 0
        e.waited = {}


def _k_barrier(self):
    engs = [self.pe, self.act, self.dve, self.pool, self.qs, self.qp, self.qa]
    cur = [(e.name, e.n, self.epoch_id) for e in engs]
    for e in [self.pe, self.act, self.dve, self.pool, self.qs]:
        for ev in cur:
            if ev[0] != e.name:
                e.wait(ev)


def _k_sb2(self, prefix, shape, dtype=BF16):
    st = self._scopes[-1] if self._scopes else self.st
    return st.enter_context(self.nc.sbuf_tensor(self.name(prefix), shape, dtype))


def _k_ps2(self, prefix, shape, dtype=F32):
    st = self._scopes[-1] if self._scopes else self.st
    return st.enter_context(self.nc.psum_tensor(self.name(prefix), shape, dtype))


K.scope = _k_scope
K.barrier = _k_barrier
K.epoch = _k_epoch
K.sb = _k_sb2
K.ps = _k_ps2
_old_init = K.__init__


def _k_init(self):
    _old_init(self)
    self._scopes = []


K.__init__ = _k_init


"""Block emitters for the megakernel. See mk.py for conventions.

Key structures:
- Shared: owns all reusable scratch (row pools packed at partitions 0/32/64/96,
  8 half-psum-banks, band tiles) so emitters don't leak SBUF/PSUM.
- emit_attention: flash attention; input provided per band by a callback
  (SBUF stream slices, or DRAM feature-major activation staged per band).
- Softmax stabilization: exact shift-invariance with per-query upper bound
  m_q = (|q_hat|^2 + max_s |k|^2)/2 (AM-GM), carried in augmented row 64 of Q
  with K row 64 = -1.
"""
import numpy as np
import ml_dtypes


BIG = 1.0e9
HD1 = HD + 1


class Bf:
    __slots__ = ("wev", "revs")

    def __init__(self):
        self.wev = None
        self.revs = []


def do(eng, method, dsts, srcs, *args, relaxed=False, **kw):
    for b in dsts:
        if not relaxed:
            eng.wait(b.wev)
        eng.wait(b.revs)
    for b in srcs:
        eng.wait(b.wev)
    ev = eng.op(method, *args, **kw)
    for b in dsts:
        b.wev = ev
        b.revs = []
    for b in srcs:
        b.revs.append(ev)
    return ev


class Row:

    def __init__(self, pool, p, b):
        self.pool = pool
        self.p = p
        self.b = b

    def __getitem__(self, sl):
        if isinstance(sl, tuple):
            sl = sl[1]
        return self.pool[self.p:self.p + 1, sl]

    @property
    def full(self):
        return self.pool[self.p:self.p + 1, :]


class Shared:
    def __init__(self, k):
        self.k = k
        self.ones_col = k.sb("onescol", [128, 1], BF16)
        self.ones_pool = k.sb("onespool", [128, 512], BF16)
        self.ones_col_b = Bf()
        self.ones_row_b = Bf()
        do(k.pool, "memset", [self.ones_col_b], [], self.ones_col[:], 1.0)
        do(k.pool, "memset", [self.ones_row_b], [], self.ones_pool[:], 1.0)
        ident = np.eye(128, dtype=ml_dtypes.bfloat16)
        self.identsb = k.sb("identsb", [128, 128], BF16)
        self.ident_b = Bf()
        do(k.qp, "dma_start", [self.ident_b], [], out=self.identsb[:],
           in_=k.const("c_ident", ident))
        self.r512 = [k.sb("r512", [1, 512], F32) for _ in range(4)]
        self.r512_b = [Bf() for _ in range(4)]
        self.bb512 = [k.sb("bb512", [1, 512], BF16) for _ in range(2)]
        self.bb512_b = [Bf(), Bf()]
        self.P = [k.ps(f"P{i}", [128, 1024]) for i in range(3)]
        self.pb = [Bf() for _ in range(6)]
        self.PT = k.ps("PT", [128, 1024], BF16)
        self.pt_b = Bf()
        self.bt = [k.sb("bt", [128, 512], BF16) for _ in range(4)]
        self.bt_b = [Bf() for _ in range(4)]
        self.t512 = [k.sb("t512", [1, 512], F32) for _ in range(2)]
        self.t512_b = [Bf(), Bf()]
        self.b512 = [k.sb("b512", [1, 512], BF16) for _ in range(2)]
        self.b512_b = [Bf(), Bf()]
        self.s11 = k.sb("s11", [1, 1], F32)
        self.s11_b = Bf()

    def psum(self, i):
        return self.P[i // 2][:, 512 * (i % 2):512 * (i % 2 + 1)], self.pb[i]

    def psum2(self, i):
        return self.P[i][:], [self.pb[2 * i], self.pb[2 * i + 1]]

    def bias_cols(self, b_ap, n, tag):
        k = self.k
        g = n // 128
        rowsb = k.sb(tag + "r", [g, 128], BF16)
        rb = Bf()
        do(k.qp, "dma_start", [rb], [], out=rowsb[:],
           in_=b_ap.rearrange("(g j) -> g j", j=128))
        do(k.pe, "transpose", [self.pt_b], [rb, self.ident_b],
           self.PT[:, 0:g], rowsb[:], self.identsb[0:g, 0:g])
        colsb = k.sb(tag + "c", [128, g], F32)
        cb = Bf()
        do(k.dve, "tensor_copy", [cb], [self.pt_b], colsb[:], self.PT[:, 0:g])
        return colsb, cb

    def head_bias_cols(self, b_ap, off, tag):
        k = self.k
        rowsb = k.sb(tag + "r", [16, 64], BF16)
        rb = Bf()
        do(k.qp, "dma_start", [rb], [], out=rowsb[:],
           in_=b_ap[off:off + 1024].rearrange("(h d) -> h d", d=64))
        do(k.pe, "transpose", [self.pt_b], [rb, self.ident_b],
           self.PT[0:64, 0:16], rowsb[:], self.identsb[0:16, 0:16])
        colsb = k.sb(tag + "c", [64, 16], F32)
        cb = Bf()
        do(k.dve, "tensor_copy", [cb], [self.pt_b], colsb[:], self.PT[0:64, 0:16])
        return colsb, cb

    def bias_row(self, b_ap, n, tag):
        k = self.k
        rsb = k.sb(tag, [1, n], BF16)
        rb = Bf()
        do(k.qp, "dma_start", [rb], [], out=rsb[:],
           in_=b_ap.rearrange("(a n) -> a n", a=1))
        return rsb, rb


def stream_load(k, dst_sb, dst_b, src_fm):
    for c in range(NCH):
        do(k.qp, "dma_start", [dst_b], [], out=dst_sb[:, c * T:(c + 1) * T],
           in_=src_fm[c * 128:(c + 1) * 128, :])


def stream_store(k, src_sb, src_b, dst_fm):
    done = Bf()
    for c in range(NCH):
        do(k.qp, "dma_start", [done], [src_b], out=dst_fm[c * 128:(c + 1) * 128, :],
           in_=src_sb[:, c * T:(c + 1) * T])
    return done


def sbuf_provider(x_sb, x_b):
    xv = x_sb.ap().rearrange("p (c t) -> p c t", c=NCH)

    def get(lo, n):
        return xv[:, :, lo:lo + n], x_b
    return get


def dram_provider(k, sh, act_dram, act_b, tag):
    stg = [k.sb(tag, [128, NCH * 512], BF16) for _ in range(2)]
    stg_b = [Bf(), Bf()]
    state = {"i": 0, "cur": None}

    def get(lo, n):
        key = (lo, n)
        if state["cur"] == key:
            i = state["i"]
            return stg[i].ap().rearrange("p (c t) -> p c t", c=NCH)[:, :, 0:n], stg_b[i]
        i = 1 - state["i"]
        state["i"] = i
        state["cur"] = key
        v = stg[i].ap().rearrange("p (c t) -> p c t", c=NCH)
        do(k.qs, "dma_start", [stg_b[i]], [act_b], out=v[:, :, 0:n],
           in_=act_dram[:, :, lo:lo + n])
        return v[:, :, 0:n], stg_b[i]
    return get



def emit_attention(k, sh, get_band, cfg, proj_evac):
    wq, wk, wv, wo = cfg["wq"], cfg["wk"], cfg["wv"], cfg["wo"]
    rope = cfg.get("rope")
    scale_q = cfg["scale_q"]
    b_row = cfg.get("b_row")
    bqk = cfg.get("bqk_cols")

    wqkv = [k.sb("wqkv", [128, 3 * NCH * HD], BF16) for _ in range(2)]
    wqkv_b = [Bf(), Bf()]
    Qa = k.sb("Qa", [HD1, T], BF16); Qa_b = Bf()
    Ka = k.sb("Ka", [HD1, T], BF16); Ka_b = Bf()
    Va = k.sb("Va", [128, NT * HD1], BF16); Va_b = Bf()
    Ph = k.sb("Ph", [128, 1024], BF16)
    Ph_b = [Bf(), Bf()]
    sqb = k.sb("sqb", [HD, 512], BF16); sqb_b = Bf()
    nk2p = k.sb("nk2p", [1, NB], F32); nk2p_b = Bf()
    kmax2 = sh.s11; kmax2_b = sh.s11_b
    lrow = sh.t512[0]; lrow_b = sh.t512_b[0]
    rrow = sh.t512[1]; rrow_b = sh.t512_b[1]
    lrow_bf = sh.b512[0]; lrow_bf_b = sh.b512_b[0]
    rrow_bf = sh.b512[1]; rrow_bf_b = sh.b512_b[1]
    bc_sb = sh.bt[0]; bc_sb_b = sh.bt_b[0]
    ybuf = k.sb("ybuf", [HD, T], BF16); ybuf_b = Bf()
    yband = k.sb("yband", [128, NCH * 512], BF16); yband_b = Bf()
    wo_sb = k.sb("wosb", [128, NCH * D], BF16); wo_b = Bf()

    if rope:
        cosQ = k.sb("cosQ", [HD, T], BF16)
        sinQ = k.sb("sinQ", [HD, T], BF16)
        cosK = k.sb("cosK", [HD, T], BF16)
        sinK = k.sb("sinK", [HD, T], BF16)
        cs_b = Bf()
        for dst_, src_ in zip((cosQ, sinQ, cosK, sinK), rope):
            do(k.qp, "dma_start", [cs_b], [], out=dst_[:], in_=src_)
        rotT = np.zeros((HD, HD), dtype=ml_dtypes.bfloat16)
        for d in range(32):
            rotT[d + 32, d] = -1.0
            rotT[d, d + 32] = 1.0
        rot_sb = k.sb("rotsb", [HD, HD], BF16); rot_b = Bf()
        do(k.qp, "dma_start", [rot_b], [], out=rot_sb[:], in_=k.const("c_rotT", rotT))
        t1 = sh.bt[1]; t1_b = sh.bt_b[1]
        t2 = sh.bt[2]; t2_b = sh.bt_b[2]

    y_dram = k.dram(k.name("ydram"), [128, NCH, T], BF16)
    y_dram_b = Bf()

    s_ps, s_bb = sh.psum2(0)
    s_half = [s_ps[:, 0:512], s_ps[:, 512:1024]]
    qv_ps, qv_b = sh.psum(2)
    avf, av_b = sh.psum(3)
    av_ps = sh.P[1][0:HD1, 512:1024]
    bc_ps, bc_b = sh.psum(4)
    if rope:
        rt_full, rt_b = sh.psum(5)
        rt_ps = sh.P[2][0:HD, 512:1024]

    do(k.pool, "memset", [Ka_b], [], Ka[HD:HD1, :], -1.0)
    Va3 = Va.ap().rearrange("p (s e) -> p s e", e=HD1)
    do(k.pool, "memset", [Va_b], [], Va3[:, :, HD:HD1], 1.0)

    def load_head_w(h, slot):
        dst = wqkv[slot].ap().rearrange("p (i c n) -> p i c n", i=3, c=NCH)
        for i, w in enumerate((wq, wk, wv)):
            do(k.qp, "dma_start", [wqkv_b[slot]], [], out=dst[:, i],
               in_=w[:, HD * h: HD * (h + 1)].rearrange("(c p) n -> p c n", p=128))
        if cfg.get("w_scale_cols") is not None:
            gcol, gcol_b = cfg["w_scale_cols"]
            for i in range(3):
                for c in range(NCH):
                    do(k.dve, "tensor_scalar", [wqkv_b[slot]], [gcol_b],
                       dst[:, i, c, :], dst[:, i, c, :],
                       gcol[:, c:c + 1], None, ALU.mult)

    do(k.qp, "dma_start", [wo_b], [], out=wo_sb[:],
       in_=wo.rearrange("(c p) n -> p c n", p=128))
    load_head_w(0, 0)

    for h in range(HEADS):
        slot = h % 2
        wb = wqkv_b[slot]
        wv3 = wqkv[slot].ap().rearrange("p (i c n) -> p i c n", i=3, c=NCH)
        if h + 1 < HEADS:
            load_head_w(h + 1, 1 - slot)

        for band in range(NB):
            sl = slice(512 * band, 512 * (band + 1))
            xb_ap, xb_b = get_band(512 * band, 512)
            for i in range(2):
                dstbuf, dstb = ((Qa, Qa_b), (Ka, Ka_b))[i]
                for c in range(NCH):
                    do(k.pe, "matmul", [qv_b], [wb, xb_b],
                       qv_ps[0:HD, :], wv3[:, i, c, :], xb_ap[:, c, :],
                       start=(c == 0), stop=(c == NCH - 1), relaxed=(c != 0))
                if rope:
                    cos, sin = (cosQ, sinQ) if i == 0 else (cosK, sinK)
                    do(k.dve, "tensor_copy", [t1_b], [qv_b], t1[0:HD, :], qv_ps[0:HD, :])
                    do(k.pe, "matmul", [rt_b], [rot_b, t1_b],
                       rt_ps, rot_sb[:], t1[0:HD, :], start=True, stop=True)
                    do(k.dve, "tensor_tensor", [t2_b], [t1_b, cs_b],
                       t2[0:HD, :], t1[0:HD, :], cos[:, sl], ALU.mult)
                    do(k.dve, "tensor_tensor", [t1_b], [rt_b, cs_b],
                       t1[0:HD, :], rt_ps, sin[:, sl], ALU.mult)
                    do(k.dve, "tensor_tensor", [dstb], [t1_b, t2_b],
                       dstbuf[0:HD, sl], t1[0:HD, :], t2[0:HD, :], ALU.add)
                elif i == 0:
                    if bqk is not None:
                        do(k.dve, "tensor_scalar", [dstb], [qv_b, bqk[1][1]],
                           dstbuf[0:HD, sl], qv_ps[0:HD, :],
                           bqk[0][0][:, h:h + 1], scale_q, ALU.add, ALU.mult)
                    else:
                        do(k.dve, "tensor_scalar", [dstb], [qv_b],
                           dstbuf[0:HD, sl], qv_ps[0:HD, :], scale_q, None, ALU.mult)
                else:
                    if bqk is not None:
                        do(k.dve, "tensor_scalar", [dstb], [qv_b, bqk[1][1]],
                           dstbuf[0:HD, sl], qv_ps[0:HD, :],
                           bqk[1][0][:, h:h + 1], None, ALU.add)
                    else:
                        do(k.dve, "tensor_copy", [dstb], [qv_b],
                           dstbuf[0:HD, sl], qv_ps[0:HD, :])
            for s in range(4 * band, 4 * band + 4):
                for c in range(NCH):
                    do(k.pe, "matmul", [qv_b], [wb, xb_b],
                       qv_ps[:, 0:HD], xb_ap[:, c, 128 * (s - 4 * band):128 * (s - 4 * band) + 128],
                       wv3[:, 2, c, :], start=(c == 0), stop=(c == NCH - 1), relaxed=(c != 0))
                do(k.dve, "tensor_copy", [Va_b], [qv_b], Va3[:, s, 0:HD], qv_ps[:, 0:HD])

        for band in range(NB):
            bsl = slice(512 * band, 512 * (band + 1))
            do(k.act, "activation", [sqb_b], [Ka_b], out=sqb[:], in_=Ka[0:HD, bsl],
               func=AF.Square)
            do(k.pe, "matmul", [bc_b], [sqb_b, sh.ones_col_b],
               bc_ps[0:1, :], sh.ones_col[0:HD, :], sqb[:], start=True, stop=True)
            do(k.dve, "tensor_reduce", [nk2p_b], [bc_b], nk2p[:, band:band + 1],
               bc_ps[0:1, :], axis=AX, op=ALU.max)
        do(k.dve, "reduce_max", [kmax2_b], [nk2p_b], kmax2[:], nk2p[:], axis=AX)
        for band in range(NB):
            bsl = slice(512 * band, 512 * (band + 1))
            do(k.act, "activation", [sqb_b], [Qa_b], out=sqb[:], in_=Qa[0:HD, bsl],
               func=AF.Square)
            do(k.pe, "matmul", [bc_b], [sqb_b, sh.ones_col_b],
               bc_ps[0:1, :], sh.ones_col[0:HD, :], sqb[:], start=True, stop=True)
            do(k.dve, "tensor_scalar", [rrow_b], [bc_b, kmax2_b],
               rrow[:], bc_ps[0:1, :], kmax2[:], 0.5, ALU.add, ALU.mult)
            do(k.qp, "dma_start", [Qa_b], [rrow_b], out=Qa[HD:HD1, bsl], in_=rrow[:])

        for band in range(NB):
            ns = 4 * (band + 1)
            for st in range(ns):
                half = st % 2
                do(k.pe, "matmul", [s_bb[half]], [Qa_b, Ka_b],
                   s_half[half], Ka[:, 128 * st:128 * (st + 1)],
                   Qa[:, 512 * band:512 * (band + 1)], start=True, stop=True)
                do(k.act, "activation", [Ph_b[half]], [s_bb[half]],
                   out=Ph[:, 512 * half:512 * (half + 1)], in_=s_half[half], func=AF.Exp)
                j = st - 4 * band
                if j >= 0:
                    do(k.pool, "affine_select", [Ph_b[half]], [],
                       Ph[:, 512 * half:512 * (half + 1)],
                       Ph[:, 512 * half:512 * (half + 1)],
                       pattern=[[1, 512]], compare_op=ALU.is_ge, fill=0.0,
                       base=-128 * j, channel_multiplier=-1)
                do(k.pe, "matmul", [av_b], [Ph_b[half], Va_b],
                   av_ps, Va3[:, st, :], Ph[:, 512 * half:512 * (half + 1)],
                   start=(st == 0), stop=(st == ns - 1), relaxed=(st != 0))
            sl = slice(512 * band, 512 * (band + 1))
            do(k.dve, "tensor_copy", [lrow_b], [av_b], lrow[:], av_ps[HD:HD1, :])
            if b_row is not None:
                do(k.dve, "tensor_copy", [lrow_bf_b], [lrow_b], lrow_bf[:], lrow[:])
                do(k.pe, "matmul", [av_b], [lrow_bf_b, b_row[1]],
                   av_ps[0:HD, :], b_row[0][0:1, 2048 + HD * h:2048 + HD * (h + 1)],
                   lrow_bf[:], start=False, stop=True)
            do(k.dve, "reciprocal", [rrow_b], [lrow_b], rrow[:], lrow[:])
            do(k.dve, "tensor_copy", [rrow_bf_b], [rrow_b], rrow_bf[:], rrow[:])
            do(k.pe, "matmul", [bc_b], [rrow_bf_b, sh.ones_row_b],
               bc_ps[0:HD, :], sh.ones_pool[0:1, 0:HD], rrow_bf[:], start=True, stop=True)
            do(k.dve, "tensor_copy", [bc_sb_b], [bc_b], bc_sb[0:HD, :], bc_ps[0:HD, :])
            do(k.dve, "tensor_tensor", [ybuf_b], [bc_sb_b, av_b],
               ybuf[:, sl], av_ps[0:HD, :], bc_sb[0:HD, :], ALU.mult)
        do(k.qs, "dma_start", [y_dram_b], [ybuf_b],
           out=y_dram[64 * (h % 2):64 * (h % 2) + 64, h // 2, :], in_=ybuf[:])

    wo3 = wo_sb.ap().rearrange("p (c n) -> p c n", c=NCH)
    for band in range(NB):
        ybv = yband.ap().rearrange("p (c n) -> p c n", c=NCH)
        do(k.qs, "dma_start", [yband_b], [y_dram_b],
           out=ybv, in_=y_dram[:, :, 512 * band:512 * (band + 1)])
        for o in range(NCH):
            pslot = o % 2
            pps, ppb = sh.psum(0 + pslot)
            for c in range(NCH):
                do(k.pe, "matmul", [ppb], [wo_b, yband_b],
                   pps, wo3[:, c, 128 * o:128 * (o + 1)], ybv[:, c, :],
                   start=(c == 0), stop=(c == NCH - 1), relaxed=(c != 0))
            proj_evac(o, band, pps, ppb)



def emit_ln(k, sh, src_sb, src_b, dst_of, eps, tag, rms=False):
    sq = sh.bt[3]; sq_b = sh.bt_b[3]
    eps11 = k.sb(tag + "eps", [1, 1], F32); eps11_b = Bf()
    do(k.pool, "memset", [eps11_b], [], eps11[:], float(eps))
    st_ps, st_bb = sh.psum2(2)
    r = sh.r512
    rb = sh.r512_b
    mrs_bf, rstd_bf = sh.bb512
    mrs_bf_b, rstd_bf_b = sh.bb512_b
    bcM = sh.bt[0]; bcM_b = sh.bt_b[0]
    bcR = sh.bt[1]; bcR_b = sh.bt_b[1]
    tmp = sh.bt[2]; tmp_b = sh.bt_b[2]
    for band in range(NB):
        for c in range(NCH):
            sl = slice(c * T + 512 * band, c * T + 512 * (band + 1))
            if not rms:
                do(k.pe, "matmul", [st_bb[0]], [src_b, sh.ones_col_b],
                   st_ps[0:1, 0:512], sh.ones_col[:], src_sb[:, sl],
                   start=(c == 0), stop=(c == NCH - 1), relaxed=(c != 0))
            do(k.act, "activation", [sq_b], [src_b], out=sq[:],
               in_=src_sb[:, sl], func=AF.Square)
            do(k.pe, "matmul", [st_bb[1]], [sq_b, sh.ones_col_b],
               st_ps[0:1, 512:1024], sh.ones_col[:], sq[:],
               start=(c == 0), stop=(c == NCH - 1), relaxed=(c != 0))
        if not rms:
            do(k.dve, "tensor_scalar", [rb[0]], [st_bb[0]], r[0][:],
               st_ps[0:1, 0:512], 1.0 / D, None, ALU.mult)
            do(k.act, "activation", [rb[1]], [rb[0]], out=r[1][:], in_=r[0][:],
               func=AF.Square)
            do(k.dve, "scalar_tensor_tensor", [rb[2]], [st_bb[1], rb[1]], r[2][:],
               st_ps[0:1, 512:1024], 1.0 / D, r[1][:], ALU.mult, ALU.subtract)
        else:
            do(k.dve, "tensor_scalar", [rb[2]], [st_bb[1]], r[2][:],
               st_ps[0:1, 512:1024], 1.0 / D, None, ALU.mult)
        do(k.act, "activation", [rb[1]], [rb[2], eps11_b], out=r[1][:], in_=r[2][:],
           func=AF.Sqrt, bias=eps11[:])
        do(k.dve, "reciprocal", [rb[3]], [rb[1]], r[3][:], r[1][:])
        do(k.dve, "tensor_copy", [rstd_bf_b], [rb[3]], rstd_bf[:], r[3][:])
        do(k.pe, "matmul", [st_bb[1]], [rstd_bf_b, sh.ones_row_b],
           st_ps[:, 512:1024], sh.ones_pool[0:1, 0:128], rstd_bf[:],
           start=True, stop=True)
        do(k.dve, "tensor_copy", [bcR_b], [st_bb[1]], bcR[:], st_ps[:, 512:1024])
        if not rms:
            do(k.dve, "tensor_tensor", [mrs_bf_b], [rb[0], rb[3]], mrs_bf[:],
               r[0][:], r[3][:], ALU.mult)
            do(k.pe, "matmul", [st_bb[0]], [mrs_bf_b, sh.ones_row_b],
               st_ps[:, 0:512], sh.ones_pool[0:1, 0:128], mrs_bf[:],
               start=True, stop=True)
            do(k.dve, "tensor_copy", [bcM_b], [st_bb[0]], bcM[:], st_ps[:, 0:512])
        for c in range(NCH):
            sl = slice(c * T + 512 * band, c * T + 512 * (band + 1))
            dres = dst_of(c, band)
            dst_ap, dst_b = dres[0], dres[1]
            post = dres[2] if len(dres) > 2 else None
            if not rms:
                do(k.dve, "tensor_tensor", [tmp_b], [src_b, bcR_b],
                   tmp[:], src_sb[:, sl], bcR[:], ALU.mult)
                do(k.dve, "tensor_tensor", [dst_b], [tmp_b, bcM_b],
                   dst_ap, tmp[:], bcM[:], ALU.subtract)
            else:
                do(k.dve, "tensor_tensor", [dst_b], [src_b, bcR_b],
                   dst_ap, src_sb[:, sl], bcR[:], ALU.mult)
            if post is not None:
                post()


def emit_postln(k, sh, acc_sb, acc_b, eps=1e-5, tag="pln"):
    def dst_of(c, band):
        sl = slice(c * T + 512 * band, c * T + 512 * (band + 1))
        return acc_sb[:, sl], acc_b
    emit_ln(k, sh, acc_sb, acc_b, dst_of, eps, tag)


def dram_act_writer(k, sh, act_dram, act_b, tag):
    stg = [k.sb(tag, [128, NCH * 512], BF16) for _ in range(2)]
    stg_b = [Bf(), Bf()]

    def dst_of(c, band):
        i = band % 2
        v = stg[i].ap().rearrange("p (c t) -> p c t", c=NCH)
        post = None
        if c == NCH - 1:
            def post():
                do(k.qs, "dma_start", [act_b], [stg_b[i]],
                   out=act_dram[:, :, 512 * band:512 * (band + 1)], in_=v)
        return v[:, c, :], stg_b[i], post
    return dst_of


def emit_gpt_mlp(k, sh, xn_sb, xn_b, w_fc, b_fc, w_mlp, b_mlp, accum_fn):
    C2 = 0.7978845608028654 * 0.044715
    C1 = 0.7978845608028654
    bfc_cols, bfc_b = sh.bias_cols(b_fc, 4096, "bfc")
    bmlp_cols, bmlp_b = sh.bias_cols(b_mlp, D, "bmlp")
    wfc_sb = k.sb("wfc", [128, NCH * 2048], BF16); wfc_b = Bf()
    wml_sb = k.sb("wml", [128, 16 * D], BF16); wml_b = Bf()
    H = k.sb("Hb", [128, 16 * 512], BF16); H_b = Bf()
    hb = sh.bt[0]; hb_b = sh.bt_b[0]
    x2 = sh.bt[1]; x2_b = sh.bt_b[1]
    u1 = sh.bt[2]; u1_b = sh.bt_b[2]
    th = sh.bt[3]; th_b = sh.bt_b[3]
    stage = k.sb("mstg", [128, NCH * 512], BF16); stage_b = Bf()
    fc_ps, fc_bb = sh.psum2(0)
    dn_ps, dn_bb = sh.psum2(1)
    for half in range(2):
        do(k.qp, "dma_start", [wfc_b], [],
           out=wfc_sb.ap().rearrange("p (c n) -> p c n", c=NCH),
           in_=w_fc[:, 2048 * half:2048 * (half + 1)].rearrange("(c p) n -> p c n", p=128))
        do(k.qp, "dma_start", [wml_b], [],
           out=wml_sb.ap().rearrange("p (c n) -> p c n", c=16),
           in_=w_mlp[2048 * half:2048 * (half + 1), :].rearrange("(c p) n -> p c n", p=128))
        wfc3 = wfc_sb.ap().rearrange("p (c n) -> p c n", c=NCH)
        wml3 = wml_sb.ap().rearrange("p (c n) -> p c n", c=16)
        H3 = H.ap().rearrange("p (c n) -> p c n", c=16)
        for band in range(NB):
            for ht in range(16):
                pp = fc_ps[:, 512 * (ht % 2):512 * (ht % 2 + 1)]
                ppb = fc_bb[ht % 2]
                for c in range(NCH):
                    do(k.pe, "matmul", [ppb], [wfc_b, xn_b],
                       pp, wfc3[:, c, 128 * ht:128 * (ht + 1)],
                       xn_sb[:, c * T + 512 * band: c * T + 512 * (band + 1)],
                       start=(c == 0), stop=(c == NCH - 1), relaxed=(c != 0))
                bcol = bfc_cols[:, 16 * half + ht:16 * half + ht + 1]
                do(k.dve, "tensor_scalar", [hb_b], [ppb, bfc_b], hb[:], pp,
                   bcol, None, ALU.add)
                do(k.act, "activation", [x2_b], [hb_b], out=x2[:], in_=hb[:],
                   func=AF.Square)
                do(k.dve, "scalar_tensor_tensor", [u1_b], [x2_b, hb_b], u1[:],
                   x2[:], C2, hb[:], ALU.mult, ALU.mult)
                do(k.dve, "scalar_tensor_tensor", [x2_b], [hb_b, u1_b], x2[:],
                   hb[:], C1, u1[:], ALU.mult, ALU.add)
                do(k.act, "activation", [th_b], [x2_b], out=th[:], in_=x2[:],
                   func=AF.Tanh)
                do(k.dve, "scalar_tensor_tensor", [H_b], [th_b, hb_b],
                   H3[:, ht, :], th[:], 1.0, hb[:], ALU.add, ALU.mult)
            stg3 = stage.ap().rearrange("p (c n) -> p c n", c=NCH)
            for o in range(NCH):
                pp = dn_ps[:, 512 * (o % 2):512 * (o % 2 + 1)]
                ppb = dn_bb[o % 2]
                for hc in range(16):
                    do(k.pe, "matmul", [ppb], [wml_b, H_b],
                       pp, wml3[:, hc, 128 * o:128 * (o + 1)], H3[:, hc, :],
                       start=(hc == 0), stop=(hc == 15), relaxed=(hc != 0))
                if half == 0:
                    do(k.dve, "scalar_tensor_tensor", [stage_b], [ppb, bmlp_b],
                       stg3[:, o, :], pp, 0.5, None, ALU.mult, ALU.bypass) \
                        if False else \
                    do(k.dve, "tensor_scalar", [stage_b], [ppb, bmlp_b],
                       stg3[:, o, :], pp, 0.5, bmlp_cols[:, o:o + 1], ALU.mult, ALU.add)
                else:
                    do(k.dve, "tensor_scalar", [stage_b], [ppb],
                       stg3[:, o, :], pp, 0.5, None, ALU.mult)
            accum_fn(band, stage, stage_b, half == 0)


def accum_stream(k, dram3, dram_b, sb, sb_b, op=ALU.add):
    for c in range(NCH):
        for j in range(0, T, 2048):
            do(k.qp, "dma_start", [dram_b], [sb_b], out=dram3[:, c, j:j + 2048],
               in_=sb[:, c * T + j:c * T + j + 2048], accum_op=op)


def emit_routing_top2(k, sh, get_band, w_router, g2cols, combine_T, combine_T_b):
    wr_sb = k.sb("wr", [128, NCH * 8], BF16); wr_b = Bf()
    do(k.qp, "dma_start", [wr_b], [],
       out=wr_sb.ap().rearrange("p (c n) -> p c n", c=NCH),
       in_=w_router.rearrange("(c p) n -> p c n", p=128))
    wr3 = wr_sb.ap().rearrange("p (c n) -> p c n", c=NCH)
    for c in range(NCH):
        do(k.dve, "tensor_scalar", [wr_b], [g2cols[1]], wr3[:, c, :], wr3[:, c, :],
           g2cols[0][:, c:c + 1], None, ALU.mult)
    iota8 = k.const("c_iota8", np.tile(np.arange(8, dtype=np.float32), (128, 1)))
    iotaB = k.const("c_iotaB", np.tile(np.arange(8, dtype=np.float32) + 1000.0, (128, 1)))
    io8 = k.sb("io8", [128, 8], F32); io8_b = Bf()
    ioB = k.sb("ioB", [128, 8], F32); ioB_b = Bf()
    do(k.qp, "dma_start", [io8_b], [], out=io8[:], in_=iota8)
    do(k.qp, "dma_start", [ioB_b], [], out=ioB[:], in_=iotaB)
    lg = k.sb("lg", [128, 8], F32); lg_b = Bf()
    lg2 = k.sb("lg2", [128, 8], F32); lg2_b = Bf()
    oh1 = k.sb("oh1", [128, 8], F32); oh1_b = Bf()
    oh2 = k.sb("oh2", [128, 8], F32); oh2_b = Bf()
    cmb = k.sb("cmb", [128, 8], BF16); cmb_b = Bf()
    m1 = k.sb("m1", [128, 1], F32); m1_b = Bf()
    m2 = k.sb("m2c", [128, 1], F32); m2_b = Bf()
    i1 = k.sb("i1", [128, 1], F32); i1_b = Bf()
    p2u = k.sb("p2u", [128, 1], F32); p2u_b = Bf()
    rden = k.sb("rden", [128, 1], F32); rden_b = Bf()
    t8 = k.sb("t8", [128, 8], F32); t8_b = Bf()
    lg_ps, lg_pb = sh.psum(0)
    for band in range(NB):
        xb_ap, xb_b = get_band(512 * band, 512)
        for j in range(4):
            tile = 4 * band + j
            for c in range(NCH):
                do(k.pe, "matmul", [lg_pb], [xb_b, wr_b],
                   lg_ps[:, 0:8], xb_ap[:, c, 128 * j:128 * (j + 1)], wr3[:, c, :],
                   start=(c == 0), stop=(c == NCH - 1), relaxed=(c != 0))
            do(k.dve, "tensor_copy", [lg_b], [lg_pb], lg[:], lg_ps[:, 0:8])
            do(k.dve, "reduce_max", [m1_b], [lg_b], m1[:], lg[:], axis=AX)
            do(k.dve, "tensor_scalar", [oh1_b], [lg_b, m1_b], oh1[:], lg[:],
               m1[:], None, ALU.is_equal)
            do(k.dve, "scalar_tensor_tensor", [t8_b], [oh1_b, ioB_b], t8[:],
               oh1[:], -1000.0, ioB[:], ALU.mult, ALU.add)
            do(k.dve, "tensor_reduce", [i1_b], [t8_b], i1[:], t8[:], axis=AX, op=ALU.min)
            do(k.dve, "tensor_scalar", [oh1_b], [io8_b, i1_b], oh1[:], io8[:],
               i1[:], None, ALU.is_equal)
            do(k.dve, "scalar_tensor_tensor", [lg2_b], [oh1_b, lg_b], lg2[:],
               oh1[:], -1.0e30, lg[:], ALU.mult, ALU.add)
            do(k.dve, "reduce_max", [m2_b], [lg2_b], m2[:], lg2[:], axis=AX)
            do(k.dve, "tensor_scalar", [oh2_b], [lg2_b, m2_b], oh2[:], lg2[:],
               m2[:], None, ALU.is_equal)
            do(k.dve, "scalar_tensor_tensor", [t8_b], [oh2_b, ioB_b], t8[:],
               oh2[:], -1000.0, ioB[:], ALU.mult, ALU.add)
            do(k.dve, "tensor_reduce", [i1_b], [t8_b], i1[:], t8[:], axis=AX, op=ALU.min)
            do(k.dve, "tensor_scalar", [oh2_b], [io8_b, i1_b], oh2[:], io8[:],
               i1[:], None, ALU.is_equal)
            do(k.dve, "tensor_tensor", [rden_b], [m1_b, m2_b], rden[:], m2[:], m1[:],
               ALU.subtract)
            do(k.act, "activation", [p2u_b], [rden_b], out=p2u[:], in_=rden[:], func=AF.Exp)
            do(k.dve, "tensor_scalar", [rden_b], [p2u_b], rden[:], p2u[:], 1.0,
               None, ALU.add)
            do(k.dve, "reciprocal", [rden_b], [rden_b], rden[:], rden[:]) \
                if False else None
            do(k.dve, "reciprocal", [i1_b], [rden_b], i1[:], rden[:])
            do(k.dve, "tensor_scalar", [t8_b], [oh2_b, p2u_b], t8[:], oh2[:],
               p2u[:], None, ALU.mult)
            do(k.dve, "tensor_tensor", [lg2_b], [t8_b, oh1_b], lg2[:], t8[:],
               oh1[:], ALU.add)
            do(k.dve, "tensor_scalar", [cmb_b], [lg2_b, i1_b], cmb[:], lg2[:],
               i1[:], None, ALU.mult)
            do(k.pe, "transpose", [sh.pt_b], [cmb_b, sh.ident_b],
               sh.PT[0:8, 0:128], cmb[:], sh.identsb[:])
            do(k.dve, "tensor_copy", [combine_T_b], [sh.pt_b],
               combine_T[:, 128 * tile:128 * (tile + 1)], sh.PT[0:8, 0:128])


def emit_moe_l0(k, sh, get_band, combine_T, combine_T_b, wg, wu, wd, g2cols,
                moe3, moe_b):
    NHC = 2752 // 128
    assert 2752 % 128 == 64
    hts = [(i * 128, 128) for i in range(21)] + [(21 * 128, 64)]
    wg_sb = k.sb("wgsb", [128, NCH * 2752], BF16); wg_b = Bf()
    wu_sb = k.sb("wusb", [128, NCH * 2752], BF16); wu_b = Bf()
    wd_sb = k.sb("wdsb", [128, 22 * D], BF16); wd_b = Bf()
    GU = k.sb("GU", [128, 22 * 512], BF16); GU_b = Bf()
    coef = k.sb("coef", [128, 512], BF16); coef_b = Bf()
    btG = sh.bt[0]; btG_b = sh.bt_b[0]
    btU = sh.bt[1]; btU_b = sh.bt_b[1]
    dstg = [k.sb("dstg", [128, 512], BF16) for _ in range(2)]
    dstg_b = [Bf(), Bf()]
    sel = k.const("c_sel8", np.kron(np.eye(8), np.ones((1, 128))).astype(ml_dtypes.bfloat16))
    sel_sb = k.sb("sel8", [8, 8 * 128], BF16); sel_b = Bf()
    do(k.qp, "dma_start", [sel_b], [], out=sel_sb[:], in_=sel)
    up_ps, up_bb = sh.psum2(0)
    dn_ps, dn_bb = sh.psum2(1)
    cf_ps, cf_pb = sh.psum(4)
    wg3 = wg_sb.ap().rearrange("p (c n) -> p c n", c=NCH)
    wu3 = wu_sb.ap().rearrange("p (c n) -> p c n", c=NCH)
    wd3 = wd_sb.ap().rearrange("p (c n) -> p c n", c=22)
    GU3 = GU.ap().rearrange("p (c n) -> p c n", c=22)

    for e in range(8):
        do(k.qp, "dma_start", [wg_b], [], out=wg3,
           in_=wg[e].rearrange("(c p) n -> p c n", p=128))
        do(k.qp, "dma_start", [wu_b], [], out=wu3,
           in_=wu[e].rearrange("(c p) n -> p c n", p=128))
        for c in range(NCH):
            do(k.dve, "tensor_scalar", [wg_b], [g2cols[1]], wg3[:, c, :],
               wg3[:, c, :], g2cols[0][:, c:c + 1], None, ALU.mult)
            do(k.dve, "tensor_scalar", [wu_b], [g2cols[1]], wu3[:, c, :],
               wu3[:, c, :], g2cols[0][:, c:c + 1], None, ALU.mult)
        do(k.qp, "dma_start", [wd_b], [],
           out=wd_sb.ap().rearrange("p (c n) -> p c n", c=22)[:, 0:21],
           in_=wd[e][0:2688, :].rearrange("(c p) n -> p c n", p=128))
        do(k.qp, "dma_start", [wd_b], [], out=wd3[0:64, 21, :], in_=wd[e][2688:2752, :])
        for band in range(NB):
            xb_ap, xb_b = get_band(512 * band, 512)
            do(k.pe, "matmul", [cf_pb], [combine_T_b, sel_b],
               cf_ps, sel_sb[:, 128 * e:128 * (e + 1)],
               combine_T[:, 512 * band:512 * (band + 1)], start=True, stop=True)
            do(k.dve, "tensor_copy", [coef_b], [cf_pb], coef[:], cf_ps)
            for hi, (h0, hn_) in enumerate(hts):
                gp = up_ps[:, 0:512]
                up_ = up_ps[:, 512:1024]
                for c in range(NCH):
                    do(k.pe, "matmul", [up_bb[0]], [wg_b, xb_b],
                       gp[0:hn_, :], wg3[:, c, h0:h0 + hn_], xb_ap[:, c, :],
                       start=(c == 0), stop=(c == NCH - 1), relaxed=(c != 0))
                    do(k.pe, "matmul", [up_bb[1]], [wu_b, xb_b],
                       up_[0:hn_, :], wu3[:, c, h0:h0 + hn_], xb_ap[:, c, :],
                       start=(c == 0), stop=(c == NCH - 1), relaxed=(c != 0))
                do(k.act, "activation", [btG_b], [up_bb[0]], out=btG[0:hn_, :],
                   in_=gp[0:hn_, :], func=AF.Silu)
                do(k.dve, "tensor_tensor", [btU_b], [up_bb[1], coef_b],
                   btU[0:hn_, :], up_[0:hn_, :], coef[0:hn_, :], ALU.mult)
                do(k.dve, "tensor_tensor", [GU_b], [btG_b, btU_b],
                   GU3[0:hn_, hi, :], btG[0:hn_, :], btU[0:hn_, :], ALU.mult)
            for o in range(NCH):
                pp = dn_ps[:, 512 * (o % 2):512 * (o % 2 + 1)]
                ppb = dn_bb[o % 2]
                for hi, (h0, hn_) in enumerate(hts):
                    do(k.pe, "matmul", [ppb], [wd_b, GU_b],
                       pp, wd3[0:hn_, hi, 128 * o:128 * (o + 1)], GU3[0:hn_, hi, :],
                       start=(hi == 0), stop=(hi == 21), relaxed=(hi != 0))
                ds = o % 2
                do(k.dve, "tensor_copy", [dstg_b[ds]], [ppb], dstg[ds][:], pp)
                do(k.qp, "dma_start", [moe_b], [dstg_b[ds]],
                   out=moe3[:, o, 512 * band:512 * (band + 1)], in_=dstg[ds][:],
                   accum_op=(ALU.bypass if e == 0 else ALU.add))


def rope_tables(scale_q):
    inv_freq = 1.0 / (10000.0 ** (np.arange(0, HD, 2, dtype=np.float32) / HD))
    t = np.arange(T, dtype=np.float32)
    freqs = np.einsum('i,j->ij', t, inv_freq)
    emb = np.concatenate([freqs, freqs], axis=-1)
    cos = np.cos(emb).T.copy()
    sin = np.sin(emb).T.copy()
    bf = ml_dtypes.bfloat16
    return (np.ascontiguousarray((cos * scale_q).astype(bf)),
            np.ascontiguousarray((sin * scale_q).astype(bf)),
            np.ascontiguousarray(cos.astype(bf)),
            np.ascontiguousarray(sin.astype(bf)))


def emit_l0ffn_block(k, sh, xin3, xin_b, xout3, xout_b, w):
    h3 = k.dram(k.name("hdram"), [128, NCH, T], BF16); h3_b = Bf()
    hn3 = k.dram(k.name("hndram"), [128, NCH, T], BF16); hn3_b = Bf()
    moe3 = k.dram(k.name("moedram"), [128, NCH, T], BF16); moe3_b = Bf()
    xhat3 = k.dram(k.name("xhat"), [128, NCH, T], BF16); xhat3_b = Bf()
    combine_T = k.sb("cmbT", [8, T], BF16); combine_T_b = Bf()
    g1cols, g1_b = sh.bias_cols(w["g1"], D, "g1c")
    g2cols, g2_b = sh.bias_cols(w["g2"], D, "g2c")
    with k.scope():
        A = k.sb("A0f", [128, NCH * T], BF16); A_b = Bf()
        for c in range(NCH):
            do(k.qs, "dma_start", [A_b], [xin_b], out=A[:, c * T:(c + 1) * T],
               in_=xin3[:, c, :])
        with k.scope():
            emit_ln(k, sh, A, A_b, dram_act_writer(k, sh, xhat3, xhat3_b, "wx"),
                    1e-6, "rms1", rms=True)

        def proj_evac(o, band, pps, ppb):
            sl = slice(o * T + 512 * band, o * T + 512 * (band + 1))
            do(k.dve, "tensor_tensor", [A_b], [ppb], A[:, sl], pps, A[:, sl], ALU.add)

        with k.scope():
            cq, sq_, ck, sk = rope_tables(0.125)
            cfg = dict(wq=w["wq"], wk=w["wk"], wv=w["wv"], wo=w["wo"], scale_q=0.125,
                       rope=(k.const("c_cosQ", cq), k.const("c_sinQ", sq_),
                             k.const("c_cosK", ck), k.const("c_sinK", sk)),
                       w_scale_cols=(g1cols, g1_b))
            emit_attention(k, sh, dram_provider(k, sh, xhat3, xhat3_b, "xhp"), cfg,
                           proj_evac)
        for c in range(NCH):
            do(k.qp, "dma_start", [h3_b], [A_b], out=h3[:, c, :],
               in_=A[:, c * T:(c + 1) * T])
        with k.scope():
            emit_ln(k, sh, A, A_b, dram_act_writer(k, sh, hn3, hn3_b, "wh"),
                    1e-6, "rms2", rms=True)
    with k.scope():
        emit_routing_top2(k, sh, dram_provider(k, sh, hn3, hn3_b, "rt"),
                          w["w_router"], (g2cols, g2_b), combine_T, combine_T_b)
    with k.scope():
        emit_moe_l0(k, sh, dram_provider(k, sh, hn3, hn3_b, "mo"),
                    combine_T, combine_T_b, w["wg"], w["wu"], w["wd"],
                    (g2cols, g2_b), moe3, moe3_b)
    import os as _os
    _tap = _os.environ.get('TAP', '')
    if _tap == 'cmb':
        do(k.qp, "dma_start", [xout_b], [combine_T_b], out=xout3[0:8, 0, :],
           in_=combine_T[:])
        return
    if _tap:
        tapd = {'h': h3, 'hn': hn3, 'moe': moe3, 'xhat': xhat3}[_tap]
        tapb = {'h': h3_b, 'hn': hn3_b, 'moe': moe3_b, 'xhat': xhat3_b}[_tap]
        for c in range(NCH):
            do(k.qp, "dma_start", [xout_b], [tapb, combine_T_b], out=xout3[:, c, :],
               in_=tapd[:, c, :])
        return
    with k.scope():
        A = k.sb("A0f2", [128, NCH * T], BF16); A_b = Bf()
        stg = k.sb("fstg", [128, NCH * 512], BF16); stg_b = Bf()
        for band in range(NB):
            v = stg.ap().rearrange("p (c t) -> p c t", c=NCH)
            bsl = slice(512 * band, 512 * (band + 1))
            do(k.qs, "dma_start", [stg_b], [xin_b], out=v, in_=xin3[:, :, bsl])
            for c in range(NCH):
                sl = slice(c * T + 512 * band, c * T + 512 * (band + 1))
                do(k.dve, "tensor_copy", [A_b], [stg_b], A[:, sl], v[:, c, :])
            do(k.qs, "dma_start", [stg_b], [h3_b, A_b], out=v, in_=h3[:, :, bsl])
            for c in range(NCH):
                sl = slice(c * T + 512 * band, c * T + 512 * (band + 1))
                do(k.dve, "tensor_tensor", [A_b], [stg_b], A[:, sl], A[:, sl],
                   v[:, c, :], ALU.add)
            do(k.qs, "dma_start", [stg_b], [moe3_b, A_b], out=v, in_=moe3[:, :, bsl])
            for c in range(NCH):
                sl = slice(c * T + 512 * band, c * T + 512 * (band + 1))
                do(k.dve, "tensor_tensor", [A_b], [stg_b], A[:, sl], A[:, sl],
                   v[:, c, :], ALU.add)
        emit_postln(k, sh, A, A_b, eps=1e-5, tag="plf")
        for c in range(NCH):
            do(k.qp, "dma_start", [xout_b], [A_b], out=xout3[:, c, :],
               in_=A[:, c * T:(c + 1) * T])


def emit_l1ffn_block(k, sh, xfull, xfull_b, xown, xown_b, prevmask, xout3, xout_b, w):
    NE = 8
    CAP = 4096.0
    moe3 = k.dram(k.name("moedr"), [128, NCH, T], BF16); moe3_b = Bf()
    combine_T = k.sb("cmbT1", [8, T], BF16); combine_T_b = Bf()
    triu = np.triu(np.ones((128, 128))).astype(np.float32)
    striu = np.triu(np.ones((128, 128)), 1).astype(np.float32)
    triu_c = k.const("c_triu", triu)
    striu_c = k.const("c_striu", striu)
    iota8 = k.const("c_iota8", np.tile(np.arange(8, dtype=np.float32), (128, 1)))
    iotaB = k.const("c_iotaB", np.tile(np.arange(8, dtype=np.float32) + 1000.0, (128, 1)))
    scratch = k.dram(k.name("rscratch"), [256, 8], F32); scratch_b = Bf()

    with k.scope():
        wg_sb = k.sb("wgl1", [128, NCH * 8], BF16); wg_b = Bf()
        do(k.qp, "dma_start", [wg_b], [],
           out=wg_sb.ap().rearrange("p (c n) -> p c n", c=NCH),
           in_=w["Wg"].rearrange("(c p) n -> p c n", p=128))
        wr3 = wg_sb.ap().rearrange("p (c n) -> p c n", c=NCH)
        trs = k.sb("trs", [128, 128], F32); trs_b = Bf()
        strs = k.sb("strs", [128, 128], F32); strs_b = Bf()
        do(k.qp, "dma_start", [trs_b], [], out=trs[:], in_=triu_c)
        do(k.qp, "dma_start", [strs_b], [], out=strs[:], in_=striu_c)
        io8 = k.sb("io8b", [128, 8], F32); io8_b = Bf()
        ioB = k.sb("ioBb", [128, 8], F32); ioB_b = Bf()
        do(k.qp, "dma_start", [io8_b], [], out=io8[:], in_=iota8)
        do(k.qp, "dma_start", [ioB_b], [], out=ioB[:], in_=iotaB)
        OH = k.sb("OHown", [128, NT * 8], F32); OH_b = Bf()
        OH3 = OH.ap().rearrange("p (t e) -> p t e", e=8)
        g1col = k.sb("g1own", [128, NT], F32); g1col_b = Bf()
        tcnt = k.sb("tcnt", [1, 2048], F32); tcnt_b = Bf()
        octile = k.sb("octile", [1, 256], F32); octile_b = Bf()
        pvmA = k.sb("pvmA", [4, 1], F32)
        pvmB = k.sb("pvmB", [4, 1], F32)
        pvm_b = Bf()
        do(k.qp, "dma_start", [pvm_b], [], out=pvmA[:],
           in_=prevmask[0:4].rearrange("(r a) -> r a", a=1))
        do(k.qp, "dma_start", [pvm_b], [], out=pvmB[:],
           in_=prevmask[4:8].rearrange("(r a) -> r a", a=1))
        lg = k.sb("lgl", [128, 8], F32); lg_b = Bf()
        m1 = k.sb("m1l", [128, 1], F32); m1_b = Bf()
        nm1 = k.sb("nm1l", [128, 1], F32); nm1_b = Bf()
        i1 = k.sb("i1l", [128, 1], F32); i1_b = Bf()
        t8 = k.sb("t8l", [128, 8], F32); t8_b = Bf()
        oh = k.sb("ohl", [128, 8], F32); oh_b = Bf()
        es = k.sb("esl", [128, 8], F32); es_b = Bf()
        zs = k.sb("zsl", [128, 1], F32); zs_b = Bf()
        stg = [k.sb("rstg", [128, NCH * 512], BF16) for _ in range(2)]
        stg_b = [Bf(), Bf()]
        lg_ps, lg_pb = sh.psum(0)
        ct_ps, ct_pb = sh.psum(2)
        def routing_pass(src_of, own):
            for rr in range(1 if own else NE):
                for band in range(NB):
                    si = band % 2
                    v = stg[si].ap().rearrange("p (c t) -> p c t", c=NCH)
                    do(k.qp, "dma_start", [stg_b[si]], [], out=v,
                       in_=src_of(rr, band))
                    for j in range(4):
                        tile = 4 * band + j
                        g = 32 * rr + tile
                        for c in range(NCH):
                            do(k.pe, "matmul", [lg_pb], [stg_b[si], wg_b],
                               lg_ps[:, 0:8], v[:, c, 128 * j:128 * (j + 1)], wr3[:, c, :],
                               start=(c == 0), stop=(c == NCH - 1), relaxed=(c != 0))
                        do(k.dve, "tensor_copy", [lg_b], [lg_pb], lg[:], lg_ps[:, 0:8])
                        do(k.dve, "reduce_max", [m1_b], [lg_b], m1[:], lg[:], axis=AX)
                        do(k.dve, "tensor_scalar", [oh_b], [lg_b, m1_b], oh[:], lg[:],
                           m1[:], None, ALU.is_equal)
                        do(k.dve, "scalar_tensor_tensor", [t8_b], [oh_b, ioB_b], t8[:],
                           oh[:], -1000.0, ioB[:], ALU.mult, ALU.add)
                        do(k.dve, "tensor_reduce", [i1_b], [t8_b], i1[:], t8[:],
                           axis=AX, op=ALU.min)
                        do(k.dve, "tensor_scalar", [oh_b], [io8_b, i1_b], oh[:], io8[:],
                           i1[:], None, ALU.is_equal)
                        do(k.pe, "matmul", [ct_pb], [oh_b, trs_b],
                           ct_ps[0:1, 0:8], trs[:, 127:128], oh[:], start=True, stop=True)
                        if not own:
                            do(k.dve, "tensor_copy", [tcnt_b], [ct_pb],
                               tcnt[0:1, 8 * g:8 * g + 8], ct_ps[0:1, 0:8])
                        else:
                            do(k.dve, "tensor_copy", [octile_b], [ct_pb],
                               octile[0:1, 8 * tile:8 * tile + 8], ct_ps[0:1, 0:8])
                            do(k.dve, "tensor_copy", [OH_b], [oh_b], OH3[:, tile, :], oh[:])
                            do(k.dve, "reduce_max", [nm1_b], [lg_b], nm1[:], lg[:],
                               axis=AX, negate=True)
                            do(k.act, "activation", [es_b], [lg_b, nm1_b], out=es[:],
                               in_=lg[:], func=AF.Exp, bias=nm1[:])
                            do(k.dve, "reduce_sum", [zs_b], [es_b], zs[:], es[:], axis=AX)
                            do(k.dve, "reciprocal", [g1col_b], [zs_b],
                               g1col[:, tile:tile + 1], zs[:])
        routing_pass(lambda rr, band: xfull[rr, :, 512 * band:512 * (band + 1)]
                     .rearrange("(c p) t -> p c t", p=128), False)
        routing_pass(lambda rr, band: xown[:, 512 * band:512 * (band + 1)]
                     .rearrange("(c p) t -> p c t", p=128), True)
        do(k.qp, "dma_start", [scratch_b], [tcnt_b],
           out=scratch.rearrange("(a g) e -> a (g e)", a=1), in_=tcnt[:])
        cnt2 = k.sb("cnt2", [128, 16], F32); cnt2_b = Bf()
        do(k.qp, "dma_start", [cnt2_b], [scratch_b],
           out=cnt2.ap().rearrange("p (a e) -> p a e", a=2),
           in_=scratch.rearrange("(a p) e -> p a e", p=128))
        rsel = np.zeros((128, 8), np.float32)
        for gg in range(128):
            rsel[gg, gg // 32] = 1.0
        rsel_c = k.const("c_rsel", rsel)
        rsel_sb = k.sb("rsel", [128, 8], F32); rsel_sb_b = Bf()
        do(k.qp, "dma_start", [rsel_sb_b], [], out=rsel_sb[:], in_=rsel_c)
        pf_ps, pf_pb = sh.psum(0)
        do(k.pe, "matmul", [pf_pb], [rsel_sb_b, cnt2_b],
           pf_ps[0:8, 0:8], rsel_sb[:, 0:8], cnt2[:, 0:8], start=True, stop=True)
        do(k.pe, "matmul", [pf_pb], [rsel_sb_b, cnt2_b],
           pf_ps[0:8, 8:16], rsel_sb[:, 0:8], cnt2[:, 8:16], start=True, stop=True)
        rankoff = k.sb("rankoff", [1, 8], F32); rankoff_b = Bf()
        ro_ps, ro_pb = sh.psum(2)
        rc2 = k.sb("rc2", [4, 16], F32); rc2_b = Bf()
        do(k.dve, "tensor_copy", [rc2_b], [pf_pb], rc2[:, :], pf_ps[0:4, 0:16])
        do(k.pe, "matmul", [ro_pb], [pvm_b, rc2_b],
           ro_ps[0:1, 0:8], pvmA[:], rc2[0:4, 0:8], start=True, stop=False)
        do(k.pe, "matmul", [ro_pb], [pvm_b, rc2_b],
           ro_ps[0:1, 0:8], pvmB[:], rc2[0:4, 8:16], start=False, stop=True)
        do(k.dve, "tensor_copy", [rankoff_b], [ro_pb], rankoff[:], ro_ps[0:1, 0:8])
        do(k.qp, "dma_start", [scratch_b], [octile_b],
           out=scratch[0:32, :].rearrange("(a g) e -> a (g e)", a=1), in_=octile[:])
        oc2 = k.sb("oc2", [32, 8], F32); oc2_b = Bf()
        do(k.qp, "dma_start", [oc2_b], [scratch_b], out=oc2[:], in_=scratch[0:32, :])
        do(k.pe, "matmul", [pf_pb], [strs_b, oc2_b],
           pf_ps[0:32, 0:8], strs[0:32, 0:32], oc2[:], start=True, stop=False)
        do(k.pe, "matmul", [pf_pb], [rankoff_b, trs_b],
           pf_ps[0:32, 0:8], trs[0:1, 0:32], rankoff[:], start=False, stop=True)
        pref = k.sb("pref", [32, 8], F32); pref_b = Bf()
        do(k.dve, "tensor_copy", [pref_b], [pf_pb], pref[:], pf_ps[0:32, 0:8])
        do(k.qp, "dma_start", [scratch_b], [pref_b], out=scratch[0:32, :], in_=pref[:])
        tbase = k.sb("tbase", [1, 256], F32); tbase_b = Bf()
        do(k.qp, "dma_start", [tbase_b], [scratch_b], out=tbase[:],
           in_=scratch[0:32, :].rearrange("(a g) e -> a (g e)", a=1))
        OHbf = OH
        p1_ps, p1_pb = sh.psum(2)
        pos = k.sb("pos", [128, 8], F32); pos_b = Bf()
        psel = k.sb("psel", [128, 1], F32); psel_b = Bf()
        keep = k.sb("keep", [128, 1], F32); keep_b = Bf()
        cmb = k.sb("cmb1", [128, 8], BF16); cmb_b = Bf()
        for tile in range(NT):
            do(k.pe, "matmul", [p1_pb], [trs_b, OH_b],
               p1_ps[:, 0:8], trs[:], OH3[:, tile, :], start=True, stop=False)
            do(k.pe, "matmul", [p1_pb], [tbase_b, trs_b],
               p1_ps[:, 0:8], trs[0:1, :], tbase[0:1, 8 * tile:8 * tile + 8],
               start=False, stop=True)
            do(k.dve, "tensor_copy", [pos_b], [p1_pb], pos[:], p1_ps[:, 0:8])
            do(k.dve, "tensor_tensor", [t8_b], [pos_b, OH_b], t8[:], pos[:],
               OH3[:, tile, :], ALU.mult)
            do(k.dve, "reduce_sum", [psel_b], [t8_b], psel[:], t8[:], axis=AX)
            do(k.dve, "tensor_scalar", [keep_b], [psel_b], keep[:], psel[:],
               CAP, None, ALU.is_le)
            do(k.dve, "tensor_tensor", [keep_b], [g1col_b], keep[:], keep[:],
               g1col[:, tile:tile + 1], ALU.mult)
            do(k.dve, "tensor_scalar", [cmb_b], [OH_b, keep_b], cmb[:],
               OH3[:, tile, :], keep[:], None, ALU.mult)
            do(k.pe, "transpose", [sh.pt_b], [cmb_b, sh.ident_b],
               sh.PT[0:8, 0:128], cmb[:], sh.identsb[:])
            do(k.dve, "tensor_copy", [combine_T_b], [sh.pt_b],
               combine_T[:, 128 * tile:128 * (tile + 1)], sh.PT[0:8, 0:128])
    with k.scope():
        sel = k.const("c_sel8", np.kron(np.eye(8), np.ones((1, 128))).astype(ml_dtypes.bfloat16))
        sel_sb = k.sb("sel81", [8, 8 * 128], BF16); sel_b = Bf()
        do(k.qp, "dma_start", [sel_b], [], out=sel_sb[:], in_=sel)
        W1h = [k.sb("W1h", [128, NCH * 2048], BF16) for _ in range(2)]
        W1_b = [Bf(), Bf()]
        W2h = [k.sb("W2h", [128, 16 * D], BF16) for _ in range(2)]
        W2_b = [Bf(), Bf()]
        H = k.sb("Hl1", [128, 16 * 512], BF16); H_b = Bf()
        coef = k.sb("coefl", [128, 512], BF16); coef_b = Bf()
        dstg = [k.sb("dsg1", [128, 512], BF16) for _ in range(2)]
        dstg_b = [Bf(), Bf()]
        stg = [k.sb("xstg1", [128, NCH * 512], BF16) for _ in range(2)]
        stg_b = [Bf(), Bf()]
        up_ps, up_bb = sh.psum2(0)
        dn_ps, dn_bb = sh.psum2(1)
        cf_ps, cf_pb = sh.psum(4)
        H3 = H.ap().rearrange("p (c n) -> p c n", c=16)

        def loadW(step, slot):
            e_, half_ = divmod(step, 2)
            do(k.qp, "dma_start", [W1_b[slot]], [],
               out=W1h[slot].ap().rearrange("p (c n) -> p c n", c=NCH),
               in_=w["W1"][e_][:, 2048 * half_:2048 * (half_ + 1)].rearrange(
                   "(c p) n -> p c n", p=128))
            do(k.qp, "dma_start", [W2_b[slot]], [],
               out=W2h[slot].ap().rearrange("p (c n) -> p c n", c=16),
               in_=w["W2"][e_][2048 * half_:2048 * (half_ + 1), :].rearrange(
                   "(c p) n -> p c n", p=128))
        loadW(0, 0)
        for e in range(NE):
            b1cols, b1_b = sh.bias_cols(w["b1"][e], 4096, k.name("b1c"))
            b2cols, b2_b = sh.bias_cols(w["b2"][e], D, k.name("b2c"))
            for half in range(2):
                step = 2 * e + half
                slot = step % 2
                if step + 1 < 2 * NE:
                    loadW(step + 1, 1 - slot)
                W13 = W1h[slot].ap().rearrange("p (c n) -> p c n", c=NCH)
                W23 = W2h[slot].ap().rearrange("p (c n) -> p c n", c=16)
                for band in range(NB):
                    si = band % 2
                    v = stg[si].ap().rearrange("p (c t) -> p c t", c=NCH)
                    do(k.qp, "dma_start", [stg_b[si]], [], out=v,
                       in_=xown[:, 512 * band:512 * (band + 1)].rearrange(
                           "(c p) t -> p c t", p=128))
                    do(k.pe, "matmul", [cf_pb], [combine_T_b, sel_b],
                       cf_ps, sel_sb[:, 128 * e:128 * (e + 1)],
                       combine_T[:, 512 * band:512 * (band + 1)], start=True, stop=True)
                    do(k.dve, "tensor_copy", [coef_b], [cf_pb], coef[:], cf_ps)
                    for ht in range(16):
                        pp = up_ps[:, 512 * (ht % 2):512 * (ht % 2 + 1)]
                        ppb = up_bb[ht % 2]
                        for c in range(NCH):
                            do(k.pe, "matmul", [ppb], [W1_b[slot], stg_b[si]],
                               pp, W13[:, c, 128 * ht:128 * (ht + 1)], v[:, c, :],
                               start=(c == 0), stop=(c == NCH - 1), relaxed=(c != 0))
                        do(k.act, "activation", [H_b], [ppb, b1_b], out=H3[:, ht, :],
                           in_=pp, func=AF.Relu,
                           bias=b1cols[:, 16 * half + ht:16 * half + ht + 1])
                    for o in range(NCH):
                        pp = dn_ps[:, 512 * (o % 2):512 * (o % 2 + 1)]
                        ppb = dn_bb[o % 2]
                        for hc in range(16):
                            do(k.pe, "matmul", [ppb], [W2_b[slot], H_b],
                               pp, W23[:, hc, 128 * o:128 * (o + 1)], H3[:, hc, :],
                               start=(hc == 0), stop=(hc == 15), relaxed=(hc != 0))
                        ds = o % 2
                        if half == 0:
                            do(k.dve, "scalar_tensor_tensor", [dstg_b[ds]],
                               [ppb, b2_b, coef_b], dstg[ds][:], pp,
                               b2cols[:, o:o + 1], coef[:], ALU.add, ALU.mult)
                        else:
                            do(k.dve, "tensor_tensor", [dstg_b[ds]], [ppb, coef_b],
                               dstg[ds][:], pp, coef[:], ALU.mult)
                        do(k.qp, "dma_start", [moe3_b], [dstg_b[ds]],
                           out=moe3[:, o, 512 * band:512 * (band + 1)],
                           in_=dstg[ds][:],
                           accum_op=(ALU.bypass if (e == 0 and half == 0) else ALU.add))
    with k.scope():
        A = k.sb("Al1f", [128, NCH * T], BF16); A_b = Bf()
        for c in range(NCH):
            do(k.qp, "dma_start", [A_b], [],
               out=A[:, c * T:(c + 1) * T],
               in_=xown[128 * c:128 * (c + 1), :])
        stg2 = k.sb("fstg1", [128, NCH * 512], BF16); stg2_b = Bf()
        for band in range(NB):
            v = stg2.ap().rearrange("p (c t) -> p c t", c=NCH)
            bsl = slice(512 * band, 512 * (band + 1))
            do(k.qs, "dma_start", [stg2_b], [moe3_b, A_b], out=v, in_=moe3[:, :, bsl])
            for c in range(NCH):
                sl = slice(c * T + 512 * band, c * T + 512 * (band + 1))
                do(k.dve, "tensor_tensor", [A_b], [stg2_b], A[:, sl], A[:, sl],
                   v[:, c, :], ALU.add)
        emit_postln(k, sh, A, A_b, eps=1e-5, tag="pl1f")
        for c in range(NCH):
            do(k.qp, "dma_start", [xout_b], [A_b], out=xout3[:, c, :],
               in_=A[:, c * T:(c + 1) * T])


from concourse.bass_utils import run_bass_kernel_spmd


def build_l0attn():
    k = K()
    x_in = k.inp("x", [D, T])
    w = dict(
        w_attn=k.inp("w_attn", [D, 3 * D]), b_attn=k.inp("b_attn", [3 * D]),
        w_proj=k.inp("w_proj", [D, D]), b_proj=k.inp("b_proj", [D]),
        w_fc=k.inp("w_fc", [D, 4 * D]), b_fc=k.inp("b_fc", [4 * D]),
        w_mlp=k.inp("w_mlp", [4 * D, D]), b_mlp=k.inp("b_mlp", [D]))
    out = k.out("o", [D, T])
    sh = Shared(k)
    A = k.sb("Astream", [128, NCH * T], BF16)
    A_b = Bf()
    stream_load(k, A, A_b, x_in)
    emit_l0attn_block(k, sh, A, A_b, w)
    done = stream_store(k, A, A_b, out)
    k.qs.wait(done.wev)
    return k


def emit_l0attn_block(k, sh, A, A_b, w):
    xacc = k.dram(k.name("xacc"), [128, NCH, T], BF16)
    xacc_b = Bf()
    for c in range(NCH):
        do(k.qp, "dma_start", [xacc_b], [A_b], out=xacc[:, c, :],
           in_=A[:, c * T:(c + 1) * T])
    xn1 = k.dram(k.name("xn1"), [128, NCH, T], BF16)
    xn1_b = Bf()
    with k.scope():
        emit_ln(k, sh, A, A_b, dram_act_writer(k, sh, xn1, xn1_b, "w1"),
                1e-10, "ln1")
    with k.scope():
        brow, brow_b = sh.bias_row(w["b_attn"], 3 * D, "battn")
        bq = sh.head_bias_cols(w["b_attn"], 0, "bq")
        bk = sh.head_bias_cols(w["b_attn"], 1024, "bk")
        bproj_cols, bproj_b = sh.bias_cols(w["b_proj"], D, "bpj")

        def proj_evac(o, band, pps, ppb):
            sl = slice(o * T + 512 * band, o * T + 512 * (band + 1))
            do(k.dve, "scalar_tensor_tensor", [A_b], [ppb, bproj_b],
               A[:, sl], pps, bproj_cols[:, o:o + 1], A[:, sl], ALU.add, ALU.add)

        cfg = dict(wq=w["w_attn"][:, 0:1024], wk=w["w_attn"][:, 1024:2048],
                   wv=w["w_attn"][:, 2048:3072], wo=w["w_proj"],
                   scale_q=0.125, b_row=(brow, brow_b), bqk_cols=(bq, bk))
        emit_attention(k, sh, dram_provider(k, sh, xn1, xn1_b, "xn1p"), cfg, proj_evac)
    accum_stream(k, xacc, xacc_b, A, A_b)
    with k.scope():
        def dst_of(c, band):
            sl = slice(c * T + 512 * band, c * T + 512 * (band + 1))
            return A[:, sl], A_b
        emit_ln(k, sh, A, A_b, dst_of, 1e-10, "ln2")
    with k.scope():
        def accum_fn(band, stage, stage_b, first):
            v = stage.ap().rearrange("p (c t) -> p c t", c=NCH)
            do(k.qp, "dma_start", [xacc_b], [stage_b],
               out=xacc[:, :, 512 * band:512 * (band + 1)], in_=v, accum_op=ALU.add)
        emit_gpt_mlp(k, sh, A, A_b, w["w_fc"], w["b_fc"], w["w_mlp"], w["b_mlp"],
                     accum_fn)
    for c in range(NCH):
        do(k.qs, "dma_start", [A_b], [xacc_b], out=A[:, c * T:(c + 1) * T],
           in_=xacc[:, c, :])
    emit_postln(k, sh, A, A_b, eps=1e-5, tag="pl0a")


def build_l0ffn():
    k = K()
    x_in = k.inp("x", [D, T])
    w = dict(
        wq=k.inp("wq", [D, D]), wk=k.inp("wk", [D, D]),
        wv=k.inp("wv", [D, D]), wo=k.inp("wo", [D, D]),
        w_router=k.inp("w_router", [D, 8]),
        wg=k.inp("wg", [8, D, 2752]), wu=k.inp("wu", [8, D, 2752]),
        wd=k.inp("wd", [8, 2752, D]),
        g1=k.inp("g1", [D]), g2=k.inp("g2", [D]))
    out = k.out("o", [D, T])
    sh = Shared(k)
    xin3 = k.dram("xin3", [128, NCH, T], BF16); xin_b = Bf()
    xout3 = k.dram("xout3", [128, NCH, T], BF16); xout_b = Bf()
    for c in range(NCH):
        do(k.qp, "dma_start", [xin_b], [], out=xin3[:, c, :],
           in_=x_in[c * 128:(c + 1) * 128, :])
    emit_l0ffn_block(k, sh, xin3, xin_b, xout3, xout_b, w)
    done = Bf()
    for c in range(NCH):
        do(k.qp, "dma_start", [done], [xout_b], out=out[c * 128:(c + 1) * 128, :],
           in_=xout3[:, c, :])
    k.qs.wait(done.wev)
    return k


def build_l1attn():
    k = K()
    x_in = k.inp("x", [D, T])
    wq = k.inp("wq", [D, D])
    wk = k.inp("wk", [D, D])
    wv = k.inp("wv", [D, D])
    wo = k.inp("wo", [D, D])
    out = k.out("o", [D, T])
    sh = Shared(k)
    x = k.sb("xstream", [128, NCH * T], BF16)
    x_b = Bf()
    stream_load(k, x, x_b, x_in)

    def proj_evac(o, band, pps, pj_b):
        sl = slice(o * T + 512 * band, o * T + 512 * (band + 1))
        do(k.dve, "tensor_tensor", [x_b], [pj_b], x[:, sl], pps, x[:, sl], ALU.add)

    cfg = dict(wq=wq, wk=wk, wv=wv, wo=wo, scale_q=0.125)
    with k.scope():
        emit_attention(k, sh, sbuf_provider(x, x_b), cfg, proj_evac)
    emit_postln(k, sh, x, x_b, eps=1e-5, tag="pl")
    done = stream_store(k, x, x_b, out)
    k.qs.wait(done.wev)
    return k


def build_l1ffn():
    k = K()
    xf = k.inp("xf", [8, D, T])
    xown = k.inp("xown", [D, T])
    pvm = k.inp("pvm", [8])
    w = dict(Wg=k.inp("Wg", [D, 8]), W1=k.inp("W1", [8, D, 4096]),
             b1=k.inp("b1", [8, 4096]), W2=k.inp("W2", [8, 4096, D]),
             b2=k.inp("b2", [8, D]))
    out = k.out("o", [D, T])
    sh = Shared(k)
    xout3 = k.dram("xout3", [128, NCH, T], BF16); xout_b = Bf()
    emit_l1ffn_block(k, sh, xf, Bf(), xown, Bf(), pvm, xout3, xout_b, w)
    done = Bf()
    for c in range(NCH):
        do(k.qp, "dma_start", [done], [xout_b], out=out[c * 128:(c + 1) * 128, :],
           in_=xout3[:, c, :])
    k.qs.wait(done.wev)
    return k


def _run(k, nc, per_core):
    maps = []
    for m in per_core:
        mm = dict(m)
        mm.update(k.consts)
        maps.append(mm)
    res = run_bass_kernel_spmd(nc, maps, core_ids=list(range(8)))
    return [res.results[c]["o"] for c in range(8)]


def kernel_4launch(**inputs):
    ct = np.ascontiguousarray
    xs = [ct(inputs["L0_attn_x"][c].T).astype(np.float32) for c in range(8)]

    k1 = build_l0attn(); nc1 = k1.finalize()
    w1 = dict(w_attn=ct(inputs["L0_attn_w_attn"].T), b_attn=inputs["L0_attn_b_attn"],
              w_proj=ct(inputs["L0_attn_w_proj"].T), b_proj=inputs["L0_attn_b_proj"],
              w_fc=ct(inputs["L0_attn_w_fc"].T), b_fc=inputs["L0_attn_b_fc"],
              w_mlp=ct(inputs["L0_attn_w_mlp_proj"].T), b_mlp=inputs["L0_attn_b_mlp_proj"])
    xs = _run(k1, nc1, [dict(w1, x=xs[c]) for c in range(8)])

    k2 = build_l0ffn(); nc2 = k2.finalize()
    w2 = dict(wq=inputs["L0_ffn_wq"], wk=inputs["L0_ffn_wk"], wv=inputs["L0_ffn_wv"],
              wo=inputs["L0_ffn_wo"], w_router=inputs["L0_ffn_w_router"],
              wg=inputs["L0_ffn_wg"], wu=inputs["L0_ffn_wu"], wd=inputs["L0_ffn_wd"],
              g1=inputs["L0_ffn_g1"], g2=inputs["L0_ffn_g2"])
    xs = _run(k2, nc2, [dict(w2, x=xs[c]) for c in range(8)])

    k3 = build_l1attn(); nc3 = k3.finalize()
    w3 = dict(wq=inputs["L1_attn_Wq"], wk=inputs["L1_attn_Wk"],
              wv=inputs["L1_attn_Wv"], wo=inputs["L1_attn_Wo"])
    xs = _run(k3, nc3, [dict(w3, x=xs[c]) for c in range(8)])

    k4 = build_l1ffn(); nc4 = k4.finalize()
    xf = np.stack(xs)
    w4 = dict(Wg=inputs["L1_ffn_Wg"], W1=inputs["L1_ffn_W1"], b1=inputs["L1_ffn_b1"],
              W2=inputs["L1_ffn_W2"], b2=inputs["L1_ffn_b2"])
    xs = _run(k4, nc4, [dict(w4, xf=xf, xown=xs[c],
                             pvm=(np.arange(8) < c).astype(np.float32))
                        for c in range(8)])
    return np.stack([xs[c].T for c in range(8)])


def emit_l1attn_block(k, sh, x, x_b, w3):
    def proj_evac(o, band, pps, pj_b):
        sl = slice(o * T + 512 * band, o * T + 512 * (band + 1))
        do(k.dve, "tensor_tensor", [x_b], [pj_b], x[:, sl], pps, x[:, sl], ALU.add)
    cfg = dict(wq=w3["wq"], wk=w3["wk"], wv=w3["wv"], wo=w3["wo"], scale_q=0.125)
    with k.scope():
        emit_attention(k, sh, sbuf_provider(x, x_b), cfg, proj_evac)
    emit_postln(k, sh, x, x_b, eps=1e-5, tag="pl")


def build_fusedA():
    k = K()
    x_in = k.inp("x", [D, T])
    w1 = dict(
        w_attn=k.inp("w_attn", [D, 3 * D]), b_attn=k.inp("b_attn", [3 * D]),
        w_proj=k.inp("w_proj", [D, D]), b_proj=k.inp("b_proj", [D]),
        w_fc=k.inp("w_fc", [D, 4 * D]), b_fc=k.inp("b_fc", [4 * D]),
        w_mlp=k.inp("w_mlp", [4 * D, D]), b_mlp=k.inp("b_mlp", [D]))
    w2 = dict(
        wq=k.inp("f_wq", [D, D]), wk=k.inp("f_wk", [D, D]),
        wv=k.inp("f_wv", [D, D]), wo=k.inp("f_wo", [D, D]),
        w_router=k.inp("w_router", [D, 8]),
        wg=k.inp("wg", [8, D, 2752]), wu=k.inp("wu", [8, D, 2752]),
        wd=k.inp("wd", [8, 2752, D]),
        g1=k.inp("g1", [D]), g2=k.inp("g2", [D]))
    w3 = dict(wq=k.inp("a_wq", [D, D]), wk=k.inp("a_wk", [D, D]),
              wv=k.inp("a_wv", [D, D]), wo=k.inp("a_wo", [D, D]))
    out = k.out("o", [D, T])
    sh = Shared(k)
    xin3 = k.dram("xin3", [128, NCH, T], BF16); xin_b = Bf()
    xout3 = k.dram("xout3", [128, NCH, T], BF16); xout_b = Bf()
    with k.scope():
        A = k.sb("AstreamA", [128, NCH * T], BF16)
        A_b = Bf()
        stream_load(k, A, A_b, x_in)
        emit_l0attn_block(k, sh, A, A_b, w1)
        for c in range(NCH):
            do(k.qp, "dma_start", [xin_b], [A_b], out=xin3[:, c, :],
               in_=A[:, c * T:(c + 1) * T])
    emit_l0ffn_block(k, sh, xin3, xin_b, xout3, xout_b, w2)
    with k.scope():
        A = k.sb("AstreamC", [128, NCH * T], BF16)
        A_b = Bf()
        for c in range(NCH):
            do(k.qs, "dma_start", [A_b], [xout_b], out=A[:, c * T:(c + 1) * T],
               in_=xout3[:, c, :])
        emit_l1attn_block(k, sh, A, A_b, w3)
        done = stream_store(k, A, A_b, out)
        k.qs.wait(done.wev)
    return k


def kernel(**inputs):
    ct = np.ascontiguousarray
    xs = [ct(inputs["L0_attn_x"][c].T).astype(np.float32) for c in range(8)]
    kA = build_fusedA(); ncA = kA.finalize()
    wA = dict(w_attn=ct(inputs["L0_attn_w_attn"].T), b_attn=inputs["L0_attn_b_attn"],
              w_proj=ct(inputs["L0_attn_w_proj"].T), b_proj=inputs["L0_attn_b_proj"],
              w_fc=ct(inputs["L0_attn_w_fc"].T), b_fc=inputs["L0_attn_b_fc"],
              w_mlp=ct(inputs["L0_attn_w_mlp_proj"].T), b_mlp=inputs["L0_attn_b_mlp_proj"],
              f_wq=inputs["L0_ffn_wq"], f_wk=inputs["L0_ffn_wk"],
              f_wv=inputs["L0_ffn_wv"], f_wo=inputs["L0_ffn_wo"],
              w_router=inputs["L0_ffn_w_router"], wg=inputs["L0_ffn_wg"],
              wu=inputs["L0_ffn_wu"], wd=inputs["L0_ffn_wd"],
              g1=inputs["L0_ffn_g1"], g2=inputs["L0_ffn_g2"],
              a_wq=inputs["L1_attn_Wq"], a_wk=inputs["L1_attn_Wk"],
              a_wv=inputs["L1_attn_Wv"], a_wo=inputs["L1_attn_Wo"])
    xs = _run(kA, ncA, [dict(wA, x=xs[c]) for c in range(8)])
    k4 = build_l1ffn(); nc4 = k4.finalize()
    xf = np.stack(xs)
    w4 = dict(Wg=inputs["L1_ffn_Wg"], W1=inputs["L1_ffn_W1"], b1=inputs["L1_ffn_b1"],
              W2=inputs["L1_ffn_W2"], b2=inputs["L1_ffn_b2"])
    xs = _run(k4, nc4, [dict(w4, xf=xf, xown=xs[c],
                             pvm=(np.arange(8) < c).astype(np.float32))
                        for c in range(8)])
    return np.stack([xs[c].T for c in range(8)])
